# version 17 (final)
# baseline (speedup 1.0000x reference)
import contextlib
import numpy as np

import concourse.bass as bass
import concourse.mybir as mybir
from concourse.bass_utils import run_bass_kernel_spmd

F32 = mybir.dt.float32
BF16 = mybir.dt.bfloat16
AF = mybir.ActivationFunctionType
ALU = mybir.AluOpType
AX = mybir.AxisListType

B, S, E = 32, 2048, 1024
H, D = 16, 64
NC = 8
BPC = B // NC
JPC = S // NC
T = 32
ET = E // 128
STL = S // 128
NCH = S // 512
NP = H // 2
EXP_SHIFT = 20.0
LN_EPS = 1e-5

ENGINES = ("sync", "gpsimd", "tensor", "vector", "scalar")

WNAMES = ["wfT", "wiT", "wcT", "ufT", "uiT", "ucT"]


class Prog:

    def __init__(self, sems):
        self.sems = sems
        self.ops = {e: [] for e in ENGINES}
        self.cnt = {name: 0 for name in sems}

    def emit(self, eng, fn, inc=None, inc_by=1):
        val = None
        if inc is not None:
            self.cnt[inc] += inc_by
            val = self.cnt[inc]
            sem = self.sems[inc]
            ib = inc_by

            def wrapped(e, fn=fn, sem=sem, ib=ib):
                ins = fn(e)
                ins.then_inc(sem, ib)
            self.ops[eng].append(wrapped)
        else:
            self.ops[eng].append(fn)
        return val

    def wait(self, eng, sem_name, val):
        sem = self.sems[sem_name]
        self.ops[eng].append(lambda e, sem=sem, val=val: e.wait_ge(sem, val))

    def wait_cur(self, eng, sem_name):
        self.wait(eng, sem_name, self.cnt[sem_name])

    def drain(self, eng):
        self.ops[eng].append(lambda e: e.drain())

    def barrier(self, tag=""):
        esem = {"sync": "bs", "gpsimd": "bg", "tensor": "bt",
                "vector": "bv", "scalar": "bc"}
        for e in ("vector", "scalar", "gpsimd"):
            self.drain(e)
        for e in ENGINES:
            self.cnt[esem[e]] += 1
            sem = self.sems[esem[e]]
            self.ops[e].append(lambda en, sem=sem: en.nop().then_inc(sem, 1))
        for e in ENGINES:
            for o in ENGINES:
                if o != e:
                    self.wait(e, esem[o], self.cnt[esem[o]])


def build_program(mode="full"):
    nc = bass.Bass("TRN2", target_bir_lowering=False, debug=False)

    din = {}

    def dram_in(name, shape):
        din[name] = nc.dram_tensor(name, list(shape), F32, kind="ExternalInput").ap()
        return din[name]

    xb = dram_in("xb", (BPC, S, E))
    xbT = dram_in("xbT", (BPC, E, S))
    xjT = dram_in("xjT", (T, E, JPC))
    for w in WNAMES + ["uoT", "woT", "WqT", "WkT", "WvTp", "WoT"]:
        dram_in(w, (E, E))
    for v in ["bf", "bi", "bc", "in_c", "gamma", "beta"]:
        dram_in(v, (128, ET))
    dram_in("bo", (1, E))
    dram_in("ident", (128, 128))
    dram_in("addmask", (128, 4 * 512))
    dram_in("ind2", (128, 2))
    dram_in("ind2t", (2, 128))
    dram_in("ones16", (16, 1))
    dram_in("gb2", (2, E))
    dram_in("ind2g", (2, E))
    dram_in("negrow", (1, S))

    if mode == "r":
        out_r = nc.dram_tensor("out_r", [JPC, E], F32, kind="ExternalOutput").ap()
        out = None
    else:
        out = nc.dram_tensor("out", [BPC, S, E], F32, kind="ExternalOutput").ap()
        out_r = None

    cc_in = nc.dram_tensor("cc_in", [JPC, E], F32, kind="Internal").ap()
    cc_out = nc.dram_tensor("cc_out", [S, E], F32, kind="Internal").ap()
    stats_dram = nc.dram_tensor("stats_dram", [2, T * JPC], F32, kind="Internal").ap()

    with contextlib.ExitStack() as stack:
        sb = lambda name, shape, dt: stack.enter_context(nc.sbuf_tensor(name, list(shape), dt))
        psum = lambda name, shape, dt=F32: stack.enter_context(nc.psum_tensor(name, list(shape), dt))

        sem_names = ["sIN", "sG", "sOUT", "sCC", "sT", "sV", "sS",
                     "rA", "rB",
                     "aXB", "aXB2", "wA", "wB", "rcA", "rcB", "xA", "xB",
                     "cg", "cg1", "cg2", "cg3", "cg4", "cg5", "cg6", "cg7", "gc", "gc1", "gc2", "gc3", "mv", "oA", "oB", "sP", "ev",
                     "bs", "bg", "bt", "bv", "bc"]
        sems = {n: stack.enter_context(nc.semaphore(n)) for n in sem_names}
        p = Prog(sems)

        bigA = psum("bigA", (128, 1024))
        bigB = psum("bigB", (128, 1024))
        ps47 = [psum(f"ps{i}", (128, 512)) for i in range(4, 8)]
        ps = [bigA[:, 0:512], bigA[:, 512:1024], bigB[:, 0:512], bigB[:, 512:1024]] + \
             [t[:, :] for t in ps47]
        sc_ps = [bigA[:, :], bigB[:, :]]

        ones_f = sb("ones_f", (1, 128), F32)
        ones128b = sb("ones128b", (128, 1), BF16)
        ind2b = sb("ind2b", (128, 2), BF16)
        ones16f = sb("ones16f", (16, 1), F32)
        ident_b = sb("ident_b", (128, 128), BF16)
        addmask_b = sb("addmask_b", (128, 4 * 512), BF16)
        ind2t_b = sb("ind2t_b", (2, 128), BF16)
        ones_bf = sb("ones_bf", (1, 128), BF16)
        gb2_b = sb("gb2_b", (2, E), BF16)
        ind2g_b = sb("ind2g_b", (2, E), BF16)

        p.emit("sync", lambda e: e.dma_start(out=ones16f[:, :], in_=din["ones16"]), inc="sIN", inc_by=16)
        p.emit("gpsimd", lambda e: e.dma_start(out=ind2b[:, :], in_=din["ind2"]), inc="sG", inc_by=16)
        p.emit("gpsimd", lambda e: e.dma_start(out=ident_b[:, :], in_=din["ident"]), inc="sG", inc_by=16)
        p.emit("gpsimd", lambda e: e.dma_start(out=addmask_b[:, :], in_=din["addmask"]), inc="sG", inc_by=16)
        p.emit("gpsimd", lambda e: e.dma_start(out=ind2t_b[:, :], in_=din["ind2t"]), inc="sG", inc_by=16)
        p.emit("gpsimd", lambda e: e.dma_start(out=gb2_b[:, :], in_=din["gb2"]), inc="sG", inc_by=16)
        p.emit("gpsimd", lambda e: e.dma_start(out=ind2g_b[:, :], in_=din["ind2g"]), inc="sG", inc_by=16)
        p.emit("vector", lambda e: e.memset(ones_f[:, :], 1.0), inc="sV")
        p.emit("vector", lambda e: e.memset(ones128b[:, :], 1.0), inc="sV")
        p.emit("vector", lambda e: e.memset(ones_bf[:, :], 1.0), inc="sV")


        rsc = stack.enter_context(contextlib.ExitStack())
        sbr = lambda name, shape, dt: rsc.enter_context(nc.sbuf_tensor(name, list(shape), dt))
        bias_f = sbr("bias_f", (128, ET), F32)
        bias_i = sbr("bias_i", (128, ET), F32)
        bias_c = sbr("bias_c", (128, ET), F32)
        incol = sbr("incol", (128, ET), F32)
        bo_row = sbr("bo_row", (1, E), F32)
        p.emit("sync", lambda e: e.dma_start(out=bias_f[:, :], in_=din["bf"]), inc="sIN", inc_by=16)
        p.emit("sync", lambda e: e.dma_start(out=bias_i[:, :], in_=din["bi"]), inc="sIN", inc_by=16)
        p.emit("sync", lambda e: e.dma_start(out=bias_c[:, :], in_=din["bc"]), inc="sIN", inc_by=16)
        p.emit("sync", lambda e: e.dma_start(out=incol[:, :], in_=din["in_c"]), inc="sIN", inc_by=16)
        p.emit("sync", lambda e: e.dma_start(out=bo_row[:, :], in_=din["bo"]), inc="sIN", inc_by=16)
        wslab = {w: sbr(f"w_{w}", (128, ET * E), BF16) for w in WNAMES}
        for w in WNAMES:
            dst = wslab[w]
            src_ = din[w].rearrange("(i p) m -> p i m", p=128)
            p.emit("gpsimd", lambda e, d=dst, s=src_: e.dma_start(
                out=d[:, :].rearrange("p (i m) -> p i m", i=ET), in_=s), inc="sG", inc_by=16)

        with contextlib.ExitStack() as rpre:
            sbp = lambda name, shape, dt: rpre.enter_context(nc.sbuf_tensor(name, list(shape), dt))
            xslab = [sbp(f"rp_xs{i}", (128, E * JPC // 128), F32) for i in range(2)]
            xbfs = [sbp(f"rp_xb{i}", (128, 2048), BF16) for i in range(2)]
            xsqs = [sbp(f"rp_xq{i}", (128, 2048), BF16) for i in range(2)]
            TH = T // 2
            mrows = sbp("rp_m", (1, TH * JPC), F32)
            qrows = sbp("rp_q", (1, TH * JPC), F32)
            trow = sbp("rp_t", (1, TH * JPC), F32)
            frow = [sbp(f"rp_f{i}", (1, 1024), F32) for i in range(2)]

            castv = {}
            sqv = {}
            statsv = {}
            foldv = {}
            pending_fold = None
            for t in range(T):
                th = t % TH
                par = t % 2
                rsem = "rA" if par == 0 else "rB"
                psm, psq = [(ps[4], ps[5]), (ps[6], ps[7]), (ps[0], ps[1]), (ps[2], ps[3])][t % 4]
                if t >= 2:
                    p.wait("sync", "sV", castv[t - 2])
                src = xjT[t].rearrange("(i p) j -> p i j", p=128)
                slabv = p.emit("sync", lambda e, s=src, d=xslab[par]: e.dma_start(
                    out=d[:, :].rearrange("p (i j) -> p i j", i=ET), in_=s), inc=rsem, inc_by=16)
                p.wait("vector", rsem, slabv)
                if t >= 2:
                    p.wait("vector", "sT", statsv[t - 2])
                p.drain("vector")
                p.emit("vector", lambda e, a=xbfs[par], b=xslab[par]: e.tensor_copy(a[:, :], b[:, :]), inc="sV")
                castv[t] = p.cnt["sV"]
                if pending_fold is not None:
                    pending_fold()
                    pending_fold = None
                p.wait("scalar", "sV", castv[t])
                if t >= 2:
                    p.wait("scalar", "sT", statsv[t - 2])
                p.emit("scalar", lambda e, a=xsqs[par], b=xbfs[par]: e.activation(
                    a[:, :], b[:, :], AF.Square), inc="sS")
                sqv[t] = p.cnt["sS"]
                p.wait("tensor", "sV", castv[t])
                if t >= 4:
                    p.wait("tensor", "sV", foldv[t - 4])
                for k in range(4):
                    p.emit("tensor", lambda e, k=k, b=xbfs[par], psm=psm: e.matmul(
                        psm[0:1, :], ones128b[:, :], b[:, 512 * k:512 * k + 512],
                        start=(k == 0), stop=(k == 3)))
                p.wait("tensor", "sS", sqv[t])
                for k in range(4):
                    p.emit("tensor", lambda e, k=k, b=xsqs[par], psq=psq: e.matmul(
                        psq[0:1, :], ones128b[:, :], b[:, 512 * k:512 * k + 512],
                        start=(k == 0), stop=(k == 3)),
                        inc=("sT" if k == 3 else None))
                statsv[t] = p.cnt["sT"]

                def emit_fold(t=t, psm=psm, psq=psq):
                    fpar = t % 2
                    p.wait("vector", "sT", statsv[t])
                    p.drain("vector")
                    p.emit("vector", lambda e, t=t, psm=psm, d=frow[fpar]: e.tensor_copy(d[:, 0:512], psm[0:1, :]))
                    p.emit("vector", lambda e, t=t, psq=psq, d=frow[fpar]: e.tensor_copy(d[:, 512:1024], psq[0:1, :]))
                    p.drain("vector")
                    th2 = t % TH
                    p.emit("vector", lambda e, th2=th2, d=frow[fpar]: e.tensor_tensor(
                        mrows[:, JPC * th2:JPC * (th2 + 1)], d[:, 0:JPC], d[:, JPC:2 * JPC], ALU.add))
                    p.emit("vector", lambda e, th2=th2, d=frow[fpar]: e.tensor_tensor(
                        qrows[:, JPC * th2:JPC * (th2 + 1)], d[:, 512:512 + JPC], d[:, 512 + JPC:512 + 2 * JPC], ALU.add), inc="sV")
                    foldv[t] = p.cnt["sV"]
                pending_fold = emit_fold
                if th == TH - 1:
                    pending_fold()
                    pending_fold = None
                    half = t // TH
                    p.drain("vector")
                    p.emit("vector", lambda e: e.tensor_scalar_mul(mrows[:, :], mrows[:, :], 1.0 / E))
                    p.drain("vector")
                    p.emit("vector", lambda e: e.tensor_tensor(trow[:, :], mrows[:, :], mrows[:, :], ALU.mult))
                    p.drain("vector")
                    p.emit("vector", lambda e: e.scalar_tensor_tensor(
                        qrows[:, :], qrows[:, :], 1.0 / E, trow[:, :], ALU.mult, ALU.subtract))
                    p.drain("vector")
                    p.emit("vector", lambda e: e.tensor_scalar_add(qrows[:, :], qrows[:, :], LN_EPS), inc="sV")
                    p.wait_cur("scalar", "sV")
                    p.emit("scalar", lambda e: e.activation(qrows[:, :], qrows[:, :], AF.Sqrt), inc="sS")
                    p.wait_cur("vector", "sS")
                    p.emit("vector", lambda e: e.reciprocal(qrows[:, :], qrows[:, :]))
                    p.drain("vector")
                    p.emit("vector", lambda e: e.tensor_tensor(trow[:, :], mrows[:, :], qrows[:, :], ALU.mult), inc="sV")
                    p.wait_cur("sync", "sV")
                    p.emit("sync", lambda e, half=half: e.dma_start(
                        out=stats_dram[0:1, TH * JPC * half:TH * JPC * (half + 1)], in_=qrows[:, :]), inc="sOUT", inc_by=16)
                    p.emit("sync", lambda e, half=half: e.dma_start(
                        out=stats_dram[1:2, TH * JPC * half:TH * JPC * (half + 1)], in_=trow[:, :]), inc="sOUT", inc_by=16)
                    if half == 0:
                        p.wait_cur("vector", "sOUT")
                    else:
                        p.wait_cur("sync", "sOUT")
            p.barrier("rpre-end")

        if True:
            uo_slab = sbr("w_uoT", (128, ET * E), BF16)
            p.emit("gpsimd", lambda e, d=uo_slab: e.dma_start(
                out=d[:, :].rearrange("p (i m) -> p i m", i=ET),
                in_=din["uoT"].rearrange("(i p) m -> p i m", p=128)), inc="sG", inc_by=16)

            cslab = [sbr(f"c_{i}", (128, ET * JPC), F32) for i in range(2)]
            cbf = [sbr(f"cb_{i}", (128, ET * JPC), BF16) for i in range(2)]
            xslab = [sbr(f"rm_xs{i}", (128, 2048), F32) for i in range(2)]
            xnsl = [sbr(f"rm_xn{i}", (128, 2048), BF16) for i in range(2)]
            strowb = [sbr(f"rm_st{i}", (128, 2 * JPC), F32) for i in range(2)]
            ntmp = [sbr(f"rm_nt{i}", (128, JPC), F32) for i in range(2)]
            gate_f = [sbr(f"g_f{i}", (128, JPC), F32) for i in range(2)]
            gate_i = [sbr(f"g_i{i}", (128, JPC), F32) for i in range(2)]
            gate_g = [sbr(f"g_g{i}", (128, JPC), F32) for i in range(2)]
            fc_t = [sbr(f"fc{i}", (128, JPC), F32) for i in range(2)]
            ig_t = [sbr(f"ig{i}", (128, JPC), F32) for i in range(2)]
            r_sb = sbr("r_sb", (128, 2 * E), F32)

            p.wait_cur("gpsimd", "sG")

            p.wait_cur("vector", "sIN")
            for m in range(ET):
                p.emit("vector", lambda e, m=m: e.tensor_copy(
                    cslab[0][:, JPC * m:JPC * (m + 1)],
                    incol[:, m:m + 1].broadcast_to([128, JPC])))
                p.emit("vector", lambda e, m=m: e.tensor_copy(
                    cbf[0][:, JPC * m:JPC * (m + 1)],
                    incol[:, m:m + 1].broadcast_to([128, JPC])),
                    inc=("sV" if m == ET - 1 else None))
            c_done_val = p.cnt["sV"]
            p.barrier("r-weights")

            evac_val = {}
            updv = {}
            xn_done = {}
            gdone_val = {}
            slab_need = {}

            def emit_slab_dma(t):
                par = t % 2
                rsem = "rA" if par == 0 else "rB"
                if t >= 2:
                    p.wait("sync", "sV", xn_done[t - 2])
                src = xjT[t].rearrange("(i p) j -> p i j", p=128)
                p.emit("sync", lambda e, s=src, d=xslab[par]: e.dma_start(
                    out=d[:, :].rearrange("p (i j) -> p i j", i=ET), in_=s), inc=rsem, inc_by=16)
                p.emit("sync", lambda e, t=t, d=strowb[par]: e.dma_start(
                    out=d[:, :].rearrange("p (r j) -> p r j", r=2),
                    in_=stats_dram[:, JPC * t:JPC * (t + 1)].unsqueeze(0).broadcast_to([128, 2, JPC])),
                    inc=rsem, inc_by=16)
                slab_need[t] = p.cnt[rsem]

            def emit_normalize(t):
                par = t % 2
                rsem = "rA" if par == 0 else "rB"
                sbv = strowb[par]
                p.wait("vector", rsem, slab_need[t])
                p.drain("vector")
                for i in range(ET):
                    tp = i % 2
                    if i >= 2:
                        p.drain("vector")
                    p.emit("vector", lambda e, i=i, tp=tp, xs=xslab[par], sbv=sbv: e.tensor_tensor(
                        ntmp[tp][:, :], xs[:, 256 * i:256 * i + 256], sbv[:, 0:JPC], ALU.mult))
                    p.drain("vector")
                    p.emit("vector", lambda e, i=i, tp=tp, xn=xnsl[par], sbv=sbv: e.tensor_tensor(
                        xn[:, 256 * i:256 * i + 256], ntmp[tp][:, :], sbv[:, JPC:2 * JPC], ALU.subtract),
                        inc=("sV" if i == ET - 1 else None))
                xn_done[t] = p.cnt["sV"]

            emit_slab_dma(0)
            emit_slab_dma(1)
            emit_normalize(0)

            for t in range(T):
                par = t % 2
                if t + 2 < T:
                    emit_slab_dma(t + 2)

                for m in range(ET):
                    mset = m % 2
                    f_ps, i_ps, g_ps = (ps[0], ps[1], ps[2]) if mset == 0 else (ps[3], ps[5], ps[6])
                    if (t, m - 2) in evac_val:
                        p.wait("tensor", "sS", evac_val[(t, m - 2)])
                    elif (t - 1, m + 6) in evac_val:
                        p.wait("tensor", "sS", evac_val[(t - 1, m + 6)])
                    if m == 0:
                        p.wait("tensor", "sV", xn_done[t])
                    for wn, pp in [("wfT", f_ps), ("wiT", i_ps), ("wcT", g_ps)]:
                        for i in range(ET):
                            p.emit("tensor", lambda e, wn=wn, pp=pp, i=i, m=m, xn=xnsl[par]: e.matmul(
                                pp[:, 0:256],
                                wslab[wn][:, E * i + 128 * m:E * i + 128 * m + 128],
                                xn[:, 256 * i:256 * i + 256],
                                start=(i == 0), stop=False))
                    if m == 0:
                        p.wait("tensor", "sV", c_done_val)
                    cb = cbf[t % 2]
                    for gi, (un, pp) in enumerate([("ufT", f_ps), ("uiT", i_ps), ("ucT", g_ps)]):
                        for i in range(ET):
                            last = (gi == 2 and i == ET - 1)
                            p.emit("tensor", lambda e, un=un, pp=pp, i=i, m=m, cb=cb: e.matmul(
                                pp[:, 0:256],
                                wslab[un][:, E * i + 128 * m:E * i + 128 * m + 128],
                                cb[:, 256 * i:256 * i + 256],
                                start=False, stop=(i == ET - 1)),
                                inc=("sT" if last else None))
                    gdone_val[(t, m)] = p.cnt["sT"]

                    p.wait("scalar", "sT", gdone_val[(t, m)])
                    if (t, m - 2) in updv:
                        p.wait("scalar", "sV", updv[(t, m - 2)])
                    elif (t - 1, m + 6) in updv:
                        p.wait("scalar", "sV", updv[(t - 1, m + 6)])
                    p.emit("scalar", lambda e, m=m, fp=f_ps, d=gate_f[mset]: e.activation(
                        d[:, :], fp[:, 0:256], AF.Sigmoid, bias=bias_f[:, m:m + 1]))
                    p.emit("scalar", lambda e, m=m, ip=i_ps, d=gate_i[mset]: e.activation(
                        d[:, :], ip[:, 0:256], AF.Sigmoid, bias=bias_i[:, m:m + 1]))
                    p.emit("scalar", lambda e, m=m, gp=g_ps, d=gate_g[mset]: e.activation(
                        d[:, :], gp[:, 0:256], AF.Tanh, bias=bias_c[:, m:m + 1]), inc="sS")
                    evac_val[(t, m)] = p.cnt["sS"]

                    cin = cslab[t % 2]
                    cout = cslab[(t + 1) % 2]
                    cbout = cbf[(t + 1) % 2]
                    p.wait("vector", "sS", evac_val[(t, m)])
                    p.drain("vector")
                    p.emit("vector", lambda e, m=m, ms=mset, ci=cin: e.tensor_tensor(
                        fc_t[ms][:, :], gate_f[ms][:, :], ci[:, JPC * m:JPC * (m + 1)], ALU.mult))
                    p.emit("vector", lambda e, ms=mset: e.tensor_tensor(
                        ig_t[ms][:, :], gate_i[ms][:, :], gate_g[ms][:, :], ALU.mult))
                    p.drain("vector")
                    p.emit("vector", lambda e, m=m, ms=mset, co=cout: e.tensor_tensor(
                        co[:, JPC * m:JPC * (m + 1)], fc_t[ms][:, :], ig_t[ms][:, :], ALU.add))
                    p.emit("vector", lambda e, m=m, ms=mset, cb2=cbout: e.tensor_tensor(
                        cb2[:, JPC * m:JPC * (m + 1)], fc_t[ms][:, :], ig_t[ms][:, :], ALU.add), inc="sV")
                    updv[(t, m)] = p.cnt["sV"]

                    if m == 1 and t + 1 < T:
                        emit_normalize(t + 1)
                c_done_val = p.cnt["sV"]

            cfin = cbf[T % 2]
            p.wait("tensor", "sV", c_done_val)
            p.wait_cur("tensor", "sS")
            for jh in range(2):
                for oc in range(2):
                    pp = ps[4 + 2 * jh + oc]
                    for i in range(ET):
                        p.emit("tensor", lambda e, jh=jh, oc=oc, i=i, pp=pp, cf=cfin: e.matmul(
                            pp[:, :],
                            cf[:, JPC * i + 128 * jh:JPC * i + 128 * jh + 128],
                            uo_slab[:, E * i + 512 * oc:E * i + 512 * oc + 512],
                            start=(i == 0), stop=False))
                    p.emit("tensor", lambda e, oc=oc, pp=pp: e.matmul(
                        pp[:, :], ones_f[:, :], bo_row[:, 512 * oc:512 * oc + 512],
                        start=False, stop=True), inc="sT")
            rdone = p.cnt["sT"]
            p.wait("vector", "sT", rdone)
            for jh in range(2):
                for oc in range(2):
                    pp = ps[4 + 2 * jh + oc]
                    p.emit("vector", lambda e, jh=jh, oc=oc, pp=pp: e.tensor_copy(
                        r_sb[:, E * jh + 512 * oc:E * jh + 512 * oc + 512], pp[:, :]), inc="sV")
            p.wait_cur("gpsimd", "sV")
            if mode == "r":
                p.emit("gpsimd", lambda e: e.dma_start(
                    out=out_r.rearrange("(jh p) e -> p jh e", p=128),
                    in_=r_sb[:, :].rearrange("p (jh e) -> p jh e", jh=2)), inc="sOUT", inc_by=16)
                p.wait_cur("gpsimd", "sOUT")
                p.barrier("end")
            else:
                p.emit("gpsimd", lambda e: e.dma_start(
                    out=cc_in.rearrange("(jh p) e -> p jh e", p=128),
                    in_=r_sb[:, :].rearrange("p (jh e) -> p jh e", jh=2)), inc="sG", inc_by=16)
                p.wait_cur("gpsimd", "sG")
                p.emit("gpsimd", lambda e: e.collective_compute(
                    "AllGather", ALU.bypass,
                    replica_groups=[list(range(NC))],
                    ins=[cc_in], outs=[cc_out]), inc="sCC")
                p.barrier("r-end")
        rsc.close()

        if mode == "r":
            _finalize(nc, p)
            return nc

        env = dict(sb=sb, ones_f=ones_f, ones128b=ones128b, ind2b=ind2b,
                   ones16f=ones16f, ident_b=ident_b,
                   addmask_b=addmask_b, ind2t_b=ind2t_b, ones_bf=ones_bf,
                   gb2_b=gb2_b, ind2g_b=ind2g_b,
                   bo_row=bo_row, sc_ps=sc_ps,
                   xb=xb, xbT=xbT, out=out, cc_out=cc_out)
        _emit_phase_a(nc, p, ps, din, env)
        _finalize(nc, p)
        return nc


def _emit_phase_a(nc, p, ps, din, env):
    sb = env["sb"]
    ones_f = env["ones_f"]
    ones128b = env["ones128b"]
    ind2b = env["ind2b"]
    ones16f = env["ones16f"]
    ident_b = env["ident_b"]
    addmask_b = env["addmask_b"]
    ind2t_b = env["ind2t_b"]
    ones_bf = env["ones_bf"]
    gb2_b = env["gb2_b"]
    ind2g_b = env["ind2g_b"]
    sc_ps = env["sc_ps"]
    xbd = env["xb"]
    xbTd = env["xbT"]
    outd = env["out"]
    cc_out = env["cc_out"]

    xnT = sb("a_xnT", (128, ET * S), BF16)
    qkvT = sb("a_qkvT", (128, NP * S), BF16)
    neg20c = sb("a_neg20", (128, 1), F32)
    p.emit("vector", lambda e: e.memset(neg20c[:, :], -EXP_SHIFT), inc="sV")

    for b in range(BPC):
        a12x = contextlib.ExitStack()
        sba12 = lambda n, s_, d: a12x.enter_context(nc.sbuf_tensor(n, list(s_), d))
        c16 = sba12(f"a_c16{b}", (16, S), F32)
        s1b = sba12(f"a_s1b{b}", (16, S), F32)
        s2b = sba12(f"a_s2b{b}", (16, S), F32)
        stg_d = [[sba12(f"a_stgd{b}_{i}_{j}", (1, 512), F32) for j in range(2)] for i in range(2)]
        stgd_sem = [["cg", "cg6"], ["cg1", "cg7"]]
        stgd_free2 = [[p.cnt["cg"], p.cnt["cg6"]], [p.cnt["cg1"], p.cnt["cg7"]]]
        sqt1 = sba12(f"a2_sq{b}", (128, S), BF16)
        sqt = [sqt1, sqt1]
        stg_s1 = [sba12(f"a2_ss1{b}_{i}", (2, 512), F32) for i in range(2)]
        stg_s2 = [sba12(f"a2_ss2{b}_{i}", (2, 512), F32) for i in range(2)]
        stg_free = [(p.cnt["cg2"], p.cnt["cg4"]), (p.cnt["cg3"], p.cnt["cg5"])]
        vsc = contextlib.ExitStack()
        vext = vsc.enter_context(nc.sbuf_tensor(f"a_vext{b}", [128, STL * H * 65], BF16))
        def emit_vproj_mms(ch, k0, k1):
            p.wait("tensor", "sG", wv_ready)
            p.wait("tensor", "sV", max(vo_ready, normv0[ch]))
            for k in range(k0, k1):
                sg = 4 * ch + k % 4
                ec = k // 4
                pp = ps[6 + k % 2]
                if (ch, k - 2) in vpev:
                    p.wait("tensor", "ev", vpev[(ch, k - 2)])
                elif (ch - 1, k + 6) in vpev:
                    p.wait("tensor", "ev", vpev[(ch - 1, k + 6)])
                for i in range(ET):
                    p.emit("tensor", lambda e, sg=sg, ec=ec, i=i, pp=pp, xnT=xnT, w=wvsh: e.matmul(
                        pp[:, :],
                        xnT[:, 2048 * i + 128 * sg:2048 * i + 128 * sg + 128],
                        w[:, E * i + 512 * ec:E * i + 512 * ec + 512],
                        start=(i == 0), stop=(i == ET - 1)),
                        inc=("sT" if i == ET - 1 else None))
                vpsT[(ch, k)] = p.cnt["sT"]
                evplan[0] += 1
                vpev[(ch, k)] = evplan[0]

        def emit_vproj_evacs(ch, k0, k1, vext=vext):
            p.wait("scalar", "sV", vo_ready)
            for k in range(k0, k1):
                sg = 4 * ch + k % 4
                ec = k // 4
                pp = ps[6 + k % 2]
                p.wait("scalar", "sT", vpsT[(ch, k)])
                p.emit("scalar", lambda e, sg=sg, ec=ec, pp=pp, vext=vext: e.activation(
                    vext[:, 1040 * sg + 65 * 8 * ec:1040 * sg + 65 * 8 * ec + 520]
                        .rearrange("p (h d) -> p h d", d=65)[:, :, 0:64],
                    pp[:, :].rearrange("p (h d) -> p h d", d=64), AF.Copy), inc="ev")
                assert p.cnt["ev"] == vpev[(ch, k)]

        with contextlib.ExitStack() as a0:
            sba = lambda n, s_, d: a0.enter_context(nc.sbuf_tensor(n, list(s_), d))
            xb1 = sba(f"a0_xbf{b}", (128, ET * 512), BF16)
            xb2 = sba(f"a0_xbf2{b}", (128, ET * 512), BF16)
            xq1 = sba(f"a0_xsq{b}", (128, ET * 512), BF16)
            xbfs = [xb1, xb2]
            xsqs = [xq1, xq1]
            wvsh = sba(f"a_wvsh{b}", (128, ET * E), BF16)
            mrow = sba(f"a0_m{b}", (1, 512), F32)
            erow = sba(f"a0_e{b}", (1, 512), F32)
            trow = sba(f"a0_t{b}", (1, 512), F32)
            nt1 = sba(f"a0_nt{b}", (128, 512), F32)
            ntmp = [nt1, nt1]
            browbf = sba(f"a0_bb{b}", (1, 1024), BF16)
            p.emit("gpsimd", lambda e, wvsh=wvsh: e.dma_start(
                out=wvsh[:, :].rearrange("p (i m) -> p i m", i=ET),
                in_=din["WvTp"].rearrange("(i p) m -> p i m", p=128)), inc="sG", inc_by=16)
            p.drain("vector")
            p.emit("vector", lambda e, vext=vext: e.memset(vext[:, 64:STL * H * 65:65], 1.0), inc="sV")
            wv_ready = p.cnt["sG"]
            vo_ready = p.cnt["sV"]
            castv0 = {}
            sq0v = {}
            statv0 = {}
            normv0 = {}
            rowev = {}
            vpev = {}
            vpsT = {}
            evplan = [p.cnt["ev"]]
            for ch in range(NCH):
                par = ch % 2
                xsem = "aXB" if par == 0 else "aXB2"
                psm, psq = ps[0], ps[1]
                psi, psv = (ps[2], ps[3]) if par == 0 else (ps[4], ps[5])
                if ch >= 2:
                    p.wait("gpsimd", "sV", normv0[ch - 2])
                src2 = xbTd[b][:, 512 * ch:512 * ch + 512].rearrange("(i p) t -> p i t", p=128)
                chv = p.emit("gpsimd", lambda e, s=src2, d=xbfs[par]: e.dma_start(
                    out=d[:, :].rearrange("p (i t) -> p i t", i=ET), in_=s), inc=xsem, inc_by=16)
                p.wait("scalar", xsem, chv)
                if ch >= 1:
                    p.wait("scalar", "sT", statv0[ch - 1])
                p.emit("scalar", lambda e, d=xsqs[par], s=xbfs[par]: e.activation(
                    d[:, :], s[:, :], AF.Square), inc="sS")
                sq0v[ch] = p.cnt["sS"]
                p.wait("tensor", xsem, chv)
                if ch >= 1:
                    p.wait("tensor", "sV", rowev[ch - 1])
                for i in range(ET):
                    p.emit("tensor", lambda e, i=i, s=xbfs[par], psm=psm: e.matmul(
                        psm[0:1, :], ones128b[:, :], s[:, 512 * i:512 * i + 512],
                        start=(i == 0), stop=(i == ET - 1)))
                p.wait("tensor", "sS", sq0v[ch])
                for i in range(ET):
                    p.emit("tensor", lambda e, i=i, s=xsqs[par], psq=psq: e.matmul(
                        psq[0:1, :], ones128b[:, :], s[:, 512 * i:512 * i + 512],
                        start=(i == 0), stop=(i == ET - 1)), inc="sT")
                statv0[ch] = p.cnt["sT"]
                p.wait("vector", "sT", statv0[ch])
                p.drain("vector")
                p.emit("vector", lambda e, mrow=mrow, psm=psm: e.tensor_scalar_mul(mrow[:, :], psm[0:1, :], 1.0 / E))
                p.emit("vector", lambda e, erow=erow, psq=psq: e.tensor_scalar_mul(erow[:, :], psq[0:1, :], 1.0 / E), inc="sV")
                rowev[ch] = p.cnt["sV"]
                p.drain("vector")
                p.emit("vector", lambda e, trow=trow, mrow=mrow: e.tensor_tensor(trow[:, :], mrow[:, :], mrow[:, :], ALU.mult))
                p.drain("vector")
                p.emit("vector", lambda e, erow=erow, trow=trow: e.tensor_tensor(erow[:, :], erow[:, :], trow[:, :], ALU.subtract))
                p.drain("vector")
                p.emit("vector", lambda e, erow=erow: e.tensor_scalar_add(erow[:, :], erow[:, :], LN_EPS), inc="sV")
                p.wait_cur("scalar", "sV")
                p.emit("scalar", lambda e, erow=erow: e.activation(erow[:, :], erow[:, :], AF.Sqrt), inc="sS")
                sqrt0v = p.cnt["sS"]
                if ch >= 1:
                    emit_vproj_mms(ch - 1, 0, 8)
                    emit_vproj_evacs(ch - 1, 0, 8)
                p.wait("vector", "sS", sqrt0v)
                p.emit("vector", lambda e, erow=erow: e.reciprocal(erow[:, :], erow[:, :]))
                p.drain("vector")
                p.emit("vector", lambda e, trow=trow, mrow=mrow, erow=erow: e.tensor_tensor(trow[:, :], mrow[:, :], erow[:, :], ALU.mult))
                p.drain("vector")
                p.emit("vector", lambda e, erow=erow, browbf=browbf: e.tensor_copy(browbf[:, 0:512], erow[:, :]))
                p.emit("vector", lambda e, trow=trow, browbf=browbf: e.tensor_copy(browbf[:, 512:1024], trow[:, :]), inc="sV")
                p.wait_cur("tensor", "sV")
                p.emit("tensor", lambda e, browbf=browbf, psi=psi: e.matmul(psi[:, :], ones_bf[:, :], browbf[:, 0:512], start=True, stop=True))
                p.emit("tensor", lambda e, browbf=browbf, psv=psv: e.matmul(psv[:, :], ones_bf[:, :], browbf[:, 512:1024], start=True, stop=True), inc="sT")
                p.wait_cur("vector", "sT")
                for i in range(ET):
                    ip = i % 2
                    p.drain("vector")
                    p.emit("vector", lambda e, i=i, d=ntmp[ip], s=xbfs[par], psi=psi: e.tensor_tensor(
                        d[:, :], s[:, 512 * i:512 * i + 512], psi[:, :], ALU.mult))
                    p.drain("vector")
                    p.emit("vector", lambda e, i=i, s=ntmp[ip], ch=ch, xnT=xnT, psv=psv: e.tensor_tensor(
                        xnT[:, 2048 * i + 512 * ch:2048 * i + 512 * ch + 512],
                        s[:, :], psv[:, :], ALU.subtract),
                        inc=("sV" if i == ET - 1 else None))
                normv0[ch] = p.cnt["sV"]
            emit_vproj_mms(NCH - 1, 0, 8)
            emit_vproj_evacs(NCH - 1, 0, 8)
            p.barrier(f"a0-{b}")

        with contextlib.ExitStack() as a12:
            with contextlib.ExitStack() as a1:
                sba1 = lambda n, s_, d: a1.enter_context(nc.sbuf_tensor(n, list(s_), d))
                qt1 = sba1(f"a_qt{b}", (128, S), BF16)
                kt1 = sba1(f"a_kt{b}", (128, S), BF16)
                qt = [qt1, qt1]
                kt = [kt1, kt1]
                ptil = sba1(f"a_ptil{b}", (128, 6 * 1024), BF16)
                wq1 = sba1(f"a_wq{b}", (128, ET * 128), BF16)
                wk1 = sba1(f"a_wk{b}", (128, ET * 128), BF16)
                wq_sl = [wq1, wq1]
                wk_sl = [wk1, wk1]


                def emit_wslice(pair):
                    par = pair % 2
                    wsem = "wA" if par == 0 else "wB"
                    if pair >= 1:
                        p.wait("gpsimd", "sT", qkproj_done[pair - 1])
                    for wn, dst in [("WqT", wq_sl[par]), ("WkT", wk_sl[par])]:
                        src = din[wn][:, 128 * pair:128 * pair + 128].rearrange(
                            "(i p) c -> p i c", p=128)
                        p.emit("gpsimd", lambda e, s=src, d=dst: e.dma_start(
                            out=d[:, :].rearrange("p (i c) -> p i c", i=ET), in_=s),
                            inc=wsem, inc_by=16)
                    wsl_need[pair] = p.cnt[wsem]

                qkproj_done = {}
                qkevac_done = {}
                wsl_need = {}
                expv = {}
                maskv = {}
                exp_guard = {0: 0, 1: 0}
                av_guard = [0, 0]
                emit_wslice(0)
                def make_stats_pieces(spair):
                    sq_done = p.cnt["sV"]
                    state = {"evac": 0}

                    def piece(ch, spair=spair):
                        cpar = ch % 2
                        s1sem = "cg2" if cpar == 0 else "cg3"
                        s2sem = "cg4" if cpar == 0 else "cg5"
                        p.wait("vector", s1sem, stg_free[cpar][0])
                        p.wait("vector", s2sem, stg_free[cpar][1])
                        p.wait("tensor", "sV", max(sq_done, state["evac"]))
                        p.emit("tensor", lambda e, spair=spair, ch=ch, qkvT=qkvT: e.matmul(
                            ps[6][0:2, :], ind2b[:, :],
                            qkvT[:, 2048 * spair + 512 * ch:2048 * spair + 512 * ch + 512],
                            start=True, stop=True))
                        p.emit("tensor", lambda e, ch=ch, s=sqt[spair % 2]: e.matmul(
                            ps[7][0:2, :], ind2b[:, :],
                            s[:, 512 * ch:512 * ch + 512],
                            start=True, stop=True), inc="sT")
                        p.wait_cur("vector", "sT")
                        p.drain("vector")
                        p.emit("vector", lambda e, d=stg_s1[cpar]: e.tensor_copy(d[:, :], ps[6][0:2, :]))
                        p.emit("vector", lambda e, d=stg_s2[cpar]: e.tensor_copy(d[:, :], ps[7][0:2, :]), inc="sV")
                        state["evac"] = p.cnt["sV"]
                        p.wait_cur("sync", "sV")
                        p.emit("sync", lambda e, spair=spair, ch=ch, d=s1b, s=stg_s1[cpar]: e.dma_start(
                            out=d[2 * spair:2 * spair + 2, 512 * ch:512 * ch + 512], in_=s[:, :]),
                            inc=s1sem, inc_by=16)
                        p.emit("sync", lambda e, spair=spair, ch=ch, d=s2b, s=stg_s2[cpar]: e.dma_start(
                            out=d[2 * spair:2 * spair + 2, 512 * ch:512 * ch + 512], in_=s[:, :]),
                            inc=s2sem, inc_by=16)
                        stg_free[cpar] = (p.cnt[s1sem], p.cnt[s2sem])
                    return piece

                pending_piece = None
                for pair in range(NP):
                    par = pair % 2
                    wsem = "wA" if par == 0 else "wB"
                    p.wait("tensor", wsem, wsl_need[pair])
                    if pair >= 1:
                        p.wait("tensor", "sV", qkevac_done[pair - 1])
                    p.wait("tensor", "sS", max(exp_guard[0], exp_guard[1]))
                    projevac = {}
                    for c in range(NCH):
                        pq = ps[c % 2]
                        pk = ps[2 + c % 2]
                        if c >= 2:
                            p.wait("tensor", "sV", projevac[c - 2])
                        for i in range(ET):
                            p.emit("tensor", lambda e, c=c, i=i, pq=pq, w=wq_sl[par]: e.matmul(
                                pq[:, :], w[:, 128 * i:128 * i + 128],
                                xnT[:, 2048 * i + 512 * c:2048 * i + 512 * c + 512],
                                start=(i == 0), stop=(i == ET - 1)),
                                inc=("sT" if i == ET - 1 else None))
                        qv = p.cnt["sT"]
                        for i in range(ET):
                            p.emit("tensor", lambda e, c=c, i=i, pk=pk, w=wk_sl[par]: e.matmul(
                                pk[:, :], w[:, 128 * i:128 * i + 128],
                                xnT[:, 2048 * i + 512 * c:2048 * i + 512 * c + 512],
                                start=(i == 0), stop=(i == ET - 1)),
                                inc=("sT" if i == ET - 1 else None))
                        kv = p.cnt["sT"]
                        if pending_piece is not None:
                            pending_piece(c)
                        p.wait("vector", "sT", qv)
                        p.emit("vector", lambda e, c=c, pq=pq, d=qt[par]: e.tensor_copy(
                            d[:, 512 * c:512 * c + 512], pq[:, :]))
                        p.wait("vector", "sT", kv)
                        p.emit("vector", lambda e, c=c, pk=pk, d=kt[par]: e.tensor_scalar_mul(
                            d[:, 512 * c:512 * c + 512], pk[:, :], 1.0 / (D ** 0.5)), inc="sV")
                        projevac[c] = p.cnt["sV"]
                    qkproj_done[pair] = p.cnt["sT"]
                    qkevac_done[pair] = p.cnt["sV"]
                    if pair + 1 < NP:
                        emit_wslice(pair + 1)

                    for c in range(NCH):
                        nst = 4 * c + 4

                        def emit_scores(sg, c=c):
                            spp = sc_ps[sg % 2]
                            diag = sg >= 4 * c
                            j = sg - 4 * c
                            p.wait("tensor", "sS", exp_guard[sg % 2])
                            if sg < 2 and c == 0:
                                p.wait("tensor", "sV", qkevac_done[pair])
                            for hh in range(2):
                                last = (hh == 1 and not diag)
                                p.emit("tensor", lambda e, sg=sg, c=c, hh=hh, spp=spp, k=kt[par], q=qt[par], diag=diag: e.matmul(
                                    spp[:, 512 * hh:512 * hh + 512],
                                    k[64 * hh:64 * hh + 64, 128 * sg:128 * sg + 128],
                                    q[64 * hh:64 * hh + 64, 512 * c:512 * c + 512],
                                    start=True, stop=not diag),
                                    inc=("sT" if last else None))
                                if diag:
                                    p.emit("tensor", lambda e, hh=hh, j=j, spp=spp: e.matmul(
                                        spp[:, 512 * hh:512 * hh + 512],
                                        ident_b[:, :], addmask_b[:, 512 * j:512 * j + 512],
                                        start=False, stop=True),
                                        inc=("sT" if hh == 1 else None))
                            slot = sg % 6
                            p.wait_cur("scalar", "sT")
                            p.emit("scalar", lambda e, slot=slot, spp=spp, ptil=ptil: e.activation(
                                ptil[:, 1024 * slot:1024 * slot + 1024], spp[:, :],
                                AF.Exp, bias=neg20c[:, 0:1]), inc="sS")
                            expv[(pair, c, sg)] = p.cnt["sS"]
                            exp_guard[sg % 2] = p.cnt["sS"]

                        def emit_av(sg, c=c, nst=nst):
                            slot = sg % 6
                            p.wait("tensor", "sS", expv[(pair, c, sg)])
                            if sg == 0 and (pair, c) != (0, 0):
                                p.wait_cur("tensor", "sV")
                            for hh in range(2):
                                p.emit("tensor", lambda e, hh=hh, sg=sg, slot=slot, pair=pair, vext=vext, ptil=ptil,
                                    st_=(sg == 0), sp_=(sg == nst - 1): e.matmul(
                                    ps[4 + hh][0:65, :],
                                    vext[:, 1040 * sg + 65 * (2 * pair + hh):1040 * sg + 65 * (2 * pair + hh) + 65],
                                    ptil[:, 1024 * slot + 512 * hh:1024 * slot + 512 * hh + 512],
                                    start=st_, stop=sp_),
                                    inc=("sT" if (hh == 1 and sg == nst - 1) else None))

                        for sg in range(nst):
                            emit_scores(sg)
                            if sg >= 3:
                                emit_av(sg - 3)
                        for tail in range(max(0, nst - 3), nst):
                            emit_av(tail)
                        avdone = p.cnt["sT"]
                        p.wait("vector", "sT", avdone)
                        for hh in range(2):
                            p.emit("vector", lambda e, hh=hh, c=c, pair=pair, qkvT=qkvT: e.tensor_copy(
                                qkvT[64 * hh:64 * hh + 64, 2048 * pair + 512 * c:2048 * pair + 512 * c + 512],
                                ps[4 + hh][0:64, :]), inc="sV")
                        cp = c % 2
                        for hh in range(2):
                            p.wait("vector", stgd_sem[hh][cp], stgd_free2[hh][cp])
                            p.emit("vector", lambda e, hh=hh, cp=cp, d=stg_d[hh][cp]: e.tensor_copy(
                                d[:, :], ps[4 + hh][64:65, :]), inc="sV")
                        p.wait_cur("sync", "sV")
                        for hh in range(2):
                            p.emit("sync", lambda e, hh=hh, c=c, pair=pair, c16=c16, s=stg_d[hh][cp]: e.dma_start(
                                out=c16[2 * pair + hh:2 * pair + hh + 1, 512 * c:512 * c + 512], in_=s[:, :]),
                                inc=stgd_sem[hh][cp], inc_by=16)
                            stgd_free2[hh][cp] = p.cnt[stgd_sem[hh][cp]]

                    pp2 = pair % 2
                    if pair >= 1:
                        p.wait_cur("vector", "sT")
                    p.drain("vector")
                    p.emit("vector", lambda e, pair=pair, d=sqt[pp2], qkvT=qkvT: e.tensor_tensor(
                        d[:, :], qkvT[:, 2048 * pair:2048 * pair + 2048],
                        qkvT[:, 2048 * pair:2048 * pair + 2048], ALU.mult), inc="sV")
                    pending_piece = make_stats_pieces(pair)
                for ch in range(NCH):
                    pending_piece(ch)
                p.barrier(f"a1-{b}")
            vsc.close()
            w3 = contextlib.ExitStack()
            wosh = w3.enter_context(nc.sbuf_tensor(f"a3_wosh{b}", [128, ET * E], BF16))
            wwsh = w3.enter_context(nc.sbuf_tensor(f"a3_wwsh{b}", [128, ET * E], BF16))
            p.emit("gpsimd", lambda e, wosh=wosh: e.dma_start(
                out=wosh[:, :].rearrange("p (i m) -> p i m", i=ET),
                in_=din["WoT"].rearrange("(i p) m -> p i m", p=128)), inc="sG", inc_by=16)
            p.emit("gpsimd", lambda e, wwsh=wwsh: e.dma_start(
                out=wwsh[:, :].rearrange("p (i m) -> p i m", i=ET),
                in_=din["woT"].rearrange("(i p) m -> p i m", p=128)), inc="sG", inc_by=16)

            with contextlib.ExitStack() as a2:
                sba2 = lambda n, s_, d: a2.enter_context(nc.sbuf_tensor(n, list(s_), d))
                scstbf2 = [sba2(f"a2_scbf2{b}_{i}", (2, S), BF16) for i in range(4)]
                minv2 = sba2(f"a2_minv2{b}", (2, S), BF16)
                scaler = sba2(f"a2_scaler{b}", (16, S), F32)
                erowbf2 = sba2(f"a2_ebf{b}", (1, S), BF16)
                minvbf = sba2(f"a2_mbf{b}", (1, S), BF16)
                mrow2 = s1b[0:1, :]
                erow2 = s2b[0:1, :]
                trow2 = scaler[0:1, :]
                ntmp = [sba2(f"a2_nt{b}_{i}", (128, 512), F32) for i in range(2)]

                p.wait_cur("vector", "cg")
                p.wait_cur("vector", "cg1")
                p.wait_cur("vector", "cg2")
                p.wait_cur("vector", "cg3")
                p.wait_cur("vector", "cg4")
                p.wait_cur("vector", "cg5")
                p.wait_cur("vector", "cg6")
                p.wait_cur("vector", "cg7")
                p.drain("vector")
                p.emit("vector", lambda e, c16=c16: e.reciprocal(c16[:, :], c16[:, :]))
                p.drain("vector")
                p.emit("vector", lambda e, s1b=s1b, c16=c16: e.tensor_tensor(s1b[:, :], s1b[:, :], c16[:, :], ALU.mult))
                p.drain("vector")
                p.emit("vector", lambda e, s2b=s2b, c16=c16: e.tensor_tensor(s2b[:, :], s2b[:, :], c16[:, :], ALU.mult))
                p.drain("vector")
                p.emit("vector", lambda e, s2b=s2b, c16=c16: e.tensor_tensor(s2b[:, :], s2b[:, :], c16[:, :], ALU.mult), inc="sV")

                p.wait_cur("tensor", "sV")
                for ch in range(NCH):
                    p.emit("tensor", lambda e, ch=ch, s=s1b: e.matmul(
                        ps[ch][0:1, :], ones16f[:, :], s[:, 512 * ch:512 * ch + 512],
                        start=True, stop=True))
                    p.emit("tensor", lambda e, ch=ch, s=s2b: e.matmul(
                        ps[4 + ch][0:1, :], ones16f[:, :], s[:, 512 * ch:512 * ch + 512],
                        start=True, stop=True), inc="sT")
                p.wait_cur("vector", "sT")
                p.drain("vector")
                for ch in range(NCH):
                    p.emit("vector", lambda e, ch=ch, mrow2=mrow2: e.tensor_scalar_mul(
                        mrow2[:, 512 * ch:512 * ch + 512], ps[ch][0:1, :], 1.0 / E))
                    p.emit("vector", lambda e, ch=ch, erow2=erow2: e.tensor_scalar_mul(
                        erow2[:, 512 * ch:512 * ch + 512], ps[4 + ch][0:1, :], 1.0 / E), inc="sV")
                p.drain("vector")
                p.emit("vector", lambda e, trow2=trow2, mrow2=mrow2: e.tensor_tensor(trow2[:, :], mrow2[:, :], mrow2[:, :], ALU.mult))
                p.drain("vector")
                p.emit("vector", lambda e, erow2=erow2, trow2=trow2: e.tensor_tensor(erow2[:, :], erow2[:, :], trow2[:, :], ALU.subtract))
                p.drain("vector")
                p.emit("vector", lambda e, erow2=erow2: e.tensor_scalar_add(erow2[:, :], erow2[:, :], LN_EPS), inc="sV")
                p.wait_cur("scalar", "sV")
                p.emit("scalar", lambda e, erow2=erow2: e.activation(erow2[:, :], erow2[:, :], AF.Sqrt), inc="sS")
                p.wait_cur("vector", "sS")
                p.emit("vector", lambda e, erow2=erow2: e.reciprocal(erow2[:, :], erow2[:, :]))
                p.drain("vector")
                p.emit("vector", lambda e, minvbf=minvbf, mrow2=mrow2, erow2=erow2: e.tensor_tensor(
                    minvbf[:, :], mrow2[:, :], erow2[:, :], ALU.mult))
                p.drain("vector")
                p.emit("vector", lambda e, erow2=erow2, erowbf2=erowbf2: e.tensor_copy(erowbf2[:, :], erow2[:, :]), inc="sV")
                p.wait_cur("tensor", "sV")
                for ch in range(NCH):
                    p.emit("tensor", lambda e, ch=ch, erowbf2=erowbf2: e.matmul(
                        ps[ch][0:16, :], ones_bf[0:1, 0:16], erowbf2[:, 512 * ch:512 * ch + 512],
                        start=True, stop=True), inc="sT")
                p.wait_cur("vector", "sT")
                for ch in range(NCH):
                    p.emit("vector", lambda e, ch=ch, scaler=scaler, c16=c16: e.tensor_tensor(
                        scaler[:, 512 * ch:512 * ch + 512], c16[:, 512 * ch:512 * ch + 512],
                        ps[ch][0:16, :], ALU.mult), inc="sV")
                scalv = p.cnt["sV"]

                p.wait_cur("sync", "sV")
                p.emit("sync", lambda e, d=minv2, s=minvbf: e.dma_start(out=d[0:1, :], in_=s[:, :]), inc="mv", inc_by=16)
                p.wait_cur("gpsimd", "sV")
                p.emit("gpsimd", lambda e, d=minv2: e.dma_start(out=d[1:2, :], in_=din["negrow"]), inc="sG", inc_by=16)
                p.wait_cur("gpsimd", "sG")
                mv2 = p.cnt["mv"]
                normv2 = {}
                scv = {}
                scbv = {}
                for pair in range(NP):
                    psl = pair % 4
                    csem = ["gc", "gc1", "gc2", "gc3"][psl]
                    p.wait("gpsimd", "sV", scalv)
                    if pair >= 4:
                        p.wait("gpsimd", "sT", scv[pair - 4])
                    p.emit("gpsimd", lambda e, pair=pair, d=scstbf2[psl], s=scaler: e.dma_start(
                        out=d[:, :], in_=s[2 * pair:2 * pair + 2, :]), inc=csem, inc_by=16)
                    scbv[pair] = p.cnt[csem]
                    for ch in range(NCH):
                        dep = 4 * pair + ch
                        pS = ps[dep % 4]
                        pT = ps[4 + dep % 4]
                        p.wait("tensor", ["gc", "gc1", "gc2", "gc3"][psl], scbv[pair])
                        if pair == 0 and ch == 0:
                            p.wait("tensor", "mv", mv2)
                            p.wait_cur("tensor", "sG")
                        if dep >= 4:
                            p.wait("tensor", "sV", normv2[dep - 4])
                        p.emit("tensor", lambda e, pair=pair, ch=ch, pS=pS, s=scstbf2[psl]: e.matmul(
                            pS[:, :], ind2g_b[:, 128 * pair:128 * pair + 128],
                            s[:, 512 * ch:512 * ch + 512],
                            start=True, stop=True))
                        p.emit("tensor", lambda e, pair=pair, ch=ch, pT=pT, m2=minv2: e.matmul(
                            pT[:, :], gb2_b[:, 128 * pair:128 * pair + 128],
                            m2[:, 512 * ch:512 * ch + 512],
                            start=True, stop=True), inc="sT")
                        scv[pair] = p.cnt["sT"]
                        p.wait_cur("vector", "sT")
                        ip = ch % 2
                        p.drain("vector")
                        p.emit("vector", lambda e, pair=pair, ch=ch, d=ntmp[ip], pS=pS, qkvT=qkvT: e.tensor_tensor(
                            d[:, :], qkvT[:, 2048 * pair + 512 * ch:2048 * pair + 512 * ch + 512],
                            pS[:, :], ALU.mult))
                        p.drain("vector")
                        p.emit("vector", lambda e, pair=pair, ch=ch, s=ntmp[ip], pT=pT, qkvT=qkvT: e.tensor_tensor(
                            qkvT[:, 2048 * pair + 512 * ch:2048 * pair + 512 * ch + 512],
                            s[:, :], pT[:, :], ALU.subtract), inc="sV")
                        normv2[dep] = p.cnt["sV"]
                p.barrier(f"a2-{b}")
        with contextlib.ExitStack() as a3:
            sba3 = lambda n, s_, d: a3.enter_context(nc.sbuf_tensor(n, list(s_), d))
            rbuf = [sba3(f"a3_r{b}_{i}", (128, 512), F32) for i in range(2)]
            xbuf = [sba3(f"a3_x{b}_{i}", (128, 512), F32) for i in range(2)]
            argb = [sba3(f"a3_a{b}_{i}", (128, 512), F32) for i in range(2)]
            hbuf = [sba3(f"a3_h{b}_{i}", (128, 512), F32) for i in range(2)]
            gbuf = [sba3(f"a3_g{b}_{i}", (128, 512), BF16) for i in range(2)]
            sgb = [sba3(f"a3_sg{b}_{i}", (128, 512), F32) for i in range(2)]
            outb = [sba3(f"a3_o{b}_{i}", (128, 1024), F32) for i in range(2)]

            p.wait_cur("tensor", "sG")

            rxv = {}
            consv = {}
            outdma = {}

            def emit_rx_dma(pre):
                ptt, pec = pre // 2, pre % 2
                ppar = pre % 2
                prsem = "rcA" if ppar == 0 else "rcB"
                pxsem = "xA" if ppar == 0 else "xB"
                if pre == 0:
                    p.wait("sync", "sCC", 1)
                if pre >= 2:
                    p.wait("sync", "sV", consv[pre - 2])
                p.emit("sync", lambda e, ptt=ptt, pec=pec, d=rbuf[ppar]: e.dma_start(
                    out=d[:, :], in_=cc_out[128 * ptt:128 * ptt + 128, 512 * pec:512 * pec + 512]),
                    inc=prsem, inc_by=16)
                p.emit("sync", lambda e, ptt=ptt, pec=pec, d=xbuf[ppar], s=xbd[b]: e.dma_start(
                    out=d[:, :], in_=s[128 * ptt:128 * ptt + 128, 512 * pec:512 * pec + 512]),
                    inc=pxsem, inc_by=16)
                rxv[pre] = (p.cnt[prsem], p.cnt[pxsem])

            emit_rx_dma(0)
            emit_rx_dma(1)
            for it in range(STL * 2):
                tt, ec = it // 2, it % 2
                bpar = it % 2
                opar = tt % 2
                osem = "oA" if opar == 0 else "oB"
                rsem = "rcA" if bpar == 0 else "rcB"
                xsem = "xA" if bpar == 0 else "xB"

                pS = ps[it % 4]
                pG = ps[4 + it % 4]
                if it >= 4:
                    p.wait("tensor", "sV", consv[it - 4])
                for i in range(ET):
                    p.emit("tensor", lambda e, tt=tt, ec=ec, i=i, pS=pS, qkvT=qkvT, wosh=wosh: e.matmul(
                        pS[:, :], qkvT[:, 2048 * i + 128 * tt:2048 * i + 128 * tt + 128],
                        wosh[:, E * i + 512 * ec:E * i + 512 * ec + 512],
                        start=(i == 0), stop=(i == ET - 1)),
                        inc=("sT" if i == ET - 1 else None))
                for i in range(ET):
                    p.emit("tensor", lambda e, tt=tt, ec=ec, i=i, pG=pG, xnT=xnT, wwsh=wwsh: e.matmul(
                        pG[:, :], xnT[:, 2048 * i + 128 * tt:2048 * i + 128 * tt + 128],
                        wwsh[:, E * i + 512 * ec:E * i + 512 * ec + 512],
                        start=(i == 0), stop=(i == ET - 1)),
                        inc=("sT" if i == ET - 1 else None))
                gdone = p.cnt["sT"]

                p.wait("vector", "sT", gdone)
                p.wait("vector", rsem, rxv[it][0])
                p.drain("vector")
                p.emit("vector", lambda e, d=argb[bpar], pG=pG, r=rbuf[bpar]: e.tensor_tensor(
                    d[:, :], pG[:, :], r[:, :], ALU.add), inc="sV")
                p.wait_cur("scalar", "sV")
                p.emit("scalar", lambda e, d=hbuf[bpar], s=argb[bpar]: e.activation(
                    d[:, :], s[:, :], AF.Tanh))
                p.drain("scalar")
                p.emit("scalar", lambda e, d=gbuf[bpar], s=hbuf[bpar]: e.activation(
                    d[:, :], s[:, :], AF.Silu), inc="sS")
                p.wait_cur("vector", "sS")
                p.wait("vector", xsem, rxv[it][1])
                p.emit("vector", lambda e, d=sgb[bpar], pS=pS, g=gbuf[bpar]: e.tensor_tensor(
                    d[:, :], pS[:, :], g[:, :], ALU.mult))
                p.drain("vector")
                if ec == 0 and tt >= 2:
                    p.wait("vector", osem, outdma[tt - 2])
                p.emit("vector", lambda e, d=outb[opar], ec=ec, s=sgb[bpar], x=xbuf[bpar]: e.tensor_tensor(
                    d[:, 512 * ec:512 * ec + 512], s[:, :],
                    x[:, :], ALU.add), inc="sV")
                consv[it] = p.cnt["sV"]

                if ec == 1:
                    p.wait("sync", "sV", consv[it])
                    p.emit("sync", lambda e, tt=tt, d=outd[b], s=outb[opar]: e.dma_start(
                        out=d[128 * tt:128 * tt + 128, :], in_=s[:, :]),
                        inc=osem, inc_by=16)
                    outdma[tt] = p.cnt[osem]
                if it + 2 < STL * 2:
                    emit_rx_dma(it + 2)
            p.wait_cur("gpsimd", "oA")
            p.wait_cur("gpsimd", "oB")
            p.barrier(f"a3-{b}")
        w3.close()
        a12x.close()

    p.wait_cur("gpsimd", "oA")
    p.wait_cur("gpsimd", "oB")
    p.barrier("end")


def _finalize(nc, p):
    with nc.Block() as block:
        for eng in ENGINES:
            def runner(e, eng=eng):
                for op in p.ops[eng]:
                    op(e)
            getattr(block, eng)(runner)



def _host_inputs(core, full):
    x = full["L0_attn_x"]
    d = {}
    d["xb"] = np.ascontiguousarray(x[BPC * core:BPC * (core + 1)])
    d["xbT"] = np.ascontiguousarray(x[BPC * core:BPC * (core + 1)].transpose(0, 2, 1))
    d["xjT"] = np.ascontiguousarray(x[:, JPC * core:JPC * (core + 1), :].transpose(0, 2, 1))
    tr = {"wfT": "L0_ffn_wf", "wiT": "L0_ffn_wi", "wcT": "L0_ffn_wc",
          "ufT": "L0_ffn_uf", "uiT": "L0_ffn_ui", "ucT": "L0_ffn_uc",
          "uoT": "L0_ffn_uo", "woT": "L0_ffn_wo",
          "WqT": "L0_attn_Wq", "WkT": "L0_attn_Wk", "WoT": "L0_attn_Wo"}
    for k, v in tr.items():
        d[k] = np.ascontiguousarray(full[v].T)
    perm = np.arange(E).reshape(D, H).T.reshape(-1)
    d["WvTp"] = np.ascontiguousarray(full["L0_attn_Wv"].T[:, perm])
    def cols(v):
        return np.ascontiguousarray(np.asarray(v).reshape(ET, 128).T)

    d["bf"] = cols(full["L0_ffn_bf"])
    d["bi"] = cols(full["L0_ffn_bi"])
    d["bc"] = cols(full["L0_ffn_bc"])
    d["in_c"] = cols(full["L0_ffn_in_c"])
    d["gamma"] = cols(full["L0_attn_gamma"])
    d["beta"] = cols(full["L0_attn_beta"])
    d["bo"] = np.asarray(full["L0_ffn_bo"]).reshape(1, E)
    sidx = np.arange(128)[:, None]
    tidx = np.arange(128)[None, :]
    trimask = (sidx <= tidx).astype(np.float32)
    d["ident"] = np.eye(128, dtype=np.float32)
    am = np.zeros((128, 4, 512), np.float32)
    for j in range(4):
        am[:, j, 0:128 * j] = -100.0
        am[:, j, 128 * j:128 * j + 128] = -100.0 * (1.0 - trimask)
    d["addmask"] = am.reshape(128, 2048)
    ind2 = np.zeros((128, 2), np.float32)
    ind2[0:64, 0] = 1.0
    ind2[64:128, 1] = 1.0
    d["ind2"] = ind2
    d["ind2t"] = np.ascontiguousarray(ind2.T)
    d["ones16"] = np.ones((16, 1), np.float32)
    gam = np.asarray(full["L0_attn_gamma"], np.float32)
    bet = np.asarray(full["L0_attn_beta"], np.float32)
    d["gb2"] = np.ascontiguousarray(np.stack([gam, bet]))
    ind2g = np.zeros((2, E), np.float32)
    for pr in range(NP):
        for hh in range(2):
            seg = slice(128 * pr + 64 * hh, 128 * pr + 64 * hh + 64)
            ind2g[hh, seg] = gam[seg]
    d["ind2g"] = ind2g
    d["negrow"] = np.full((1, S), -1.0, np.float32)
    return {k: np.ascontiguousarray(v.astype(np.float32)) for k, v in d.items()}


def kernel(**inputs):
    nc = build_program("full")
    in_maps = [_host_inputs(c, inputs) for c in range(NC)]
    res = run_bass_kernel_spmd(nc, in_maps, core_ids=list(range(NC)))
    return np.concatenate([np.asarray(r["out"]) for r in res.results], axis=0)


# version 18
# speedup vs baseline: 1.0212x; 1.0212x over previous
import contextlib
import numpy as np

import concourse.bass as bass
import concourse.mybir as mybir
from concourse.bass_utils import run_bass_kernel_spmd

F32 = mybir.dt.float32
BF16 = mybir.dt.bfloat16
AF = mybir.ActivationFunctionType
ALU = mybir.AluOpType
AX = mybir.AxisListType

B, S, E = 32, 2048, 1024
H, D = 16, 64
NC = 8
BPC = B // NC
JPC = S // NC
T = 32
ET = E // 128
STL = S // 128
NCH = S // 512
NP = H // 2
EXP_SHIFT = 20.0
LN_EPS = 1e-5

ENGINES = ("sync", "gpsimd", "tensor", "vector", "scalar")

WNAMES = ["wfT", "wiT", "wcT", "ufT", "uiT", "ucT"]


class Prog:

    def __init__(self, sems):
        self.sems = sems
        self.ops = {e: [] for e in ENGINES}
        self.cnt = {name: 0 for name in sems}

    def emit(self, eng, fn, inc=None, inc_by=1):
        val = None
        if inc is not None:
            self.cnt[inc] += inc_by
            val = self.cnt[inc]
            sem = self.sems[inc]
            ib = inc_by

            def wrapped(e, fn=fn, sem=sem, ib=ib):
                ins = fn(e)
                ins.then_inc(sem, ib)
            self.ops[eng].append(wrapped)
        else:
            self.ops[eng].append(fn)
        return val

    def wait(self, eng, sem_name, val):
        sem = self.sems[sem_name]
        self.ops[eng].append(lambda e, sem=sem, val=val: e.wait_ge(sem, val))

    def wait_cur(self, eng, sem_name):
        self.wait(eng, sem_name, self.cnt[sem_name])

    def drain(self, eng):
        self.ops[eng].append(lambda e: e.drain())

    def barrier(self, tag=""):
        esem = {"sync": "bs", "gpsimd": "bg", "tensor": "bt",
                "vector": "bv", "scalar": "bc"}
        for e in ("vector", "scalar", "gpsimd"):
            self.drain(e)
        for e in ENGINES:
            self.cnt[esem[e]] += 1
            sem = self.sems[esem[e]]
            self.ops[e].append(lambda en, sem=sem: en.nop().then_inc(sem, 1))
        for e in ENGINES:
            for o in ENGINES:
                if o != e:
                    self.wait(e, esem[o], self.cnt[esem[o]])


def build_program(mode="full"):
    nc = bass.Bass("TRN2", target_bir_lowering=False, debug=False)

    din = {}

    def dram_in(name, shape):
        din[name] = nc.dram_tensor(name, list(shape), F32, kind="ExternalInput").ap()
        return din[name]

    xb = dram_in("xb", (BPC, S, E))
    xbT = dram_in("xbT", (BPC, E, S))
    xjT = dram_in("xjT", (T, E, JPC))
    for w in WNAMES + ["uoT", "woT", "WqT", "WkT", "WvTp", "WoT"]:
        dram_in(w, (E, E))
    for v in ["bf", "bi", "bc", "in_c", "gamma", "beta"]:
        dram_in(v, (128, ET))
    dram_in("bo", (1, E))
    dram_in("ident", (128, 128))
    dram_in("addmask", (128, 4 * 512))
    dram_in("ind2", (128, 2))
    dram_in("ind2t", (2, 128))
    dram_in("ones16", (16, 1))
    dram_in("gb2", (2, E))
    dram_in("ind2g", (2, E))
    dram_in("negrow", (1, S))

    if mode == "r":
        out_r = nc.dram_tensor("out_r", [JPC, E], F32, kind="ExternalOutput").ap()
        out = None
    else:
        out = nc.dram_tensor("out", [BPC, S, E], F32, kind="ExternalOutput").ap()
        out_r = None

    cc_in = nc.dram_tensor("cc_in", [JPC, E], F32, kind="Internal").ap()
    cc_out = nc.dram_tensor("cc_out", [S, E], F32, kind="Internal").ap()
    stats_dram = nc.dram_tensor("stats_dram", [2, T * JPC], F32, kind="Internal").ap()

    with contextlib.ExitStack() as stack:
        sb = lambda name, shape, dt: stack.enter_context(nc.sbuf_tensor(name, list(shape), dt))
        psum = lambda name, shape, dt=F32: stack.enter_context(nc.psum_tensor(name, list(shape), dt))

        sem_names = ["sIN", "sG", "sOUT", "sCC", "sT", "sV", "sS",
                     "rA", "rB",
                     "aXB", "aXB2", "wA", "wB", "rcA", "rcB", "xA", "xB",
                     "cg", "cg1", "cg2", "cg3", "cg4", "cg5", "cg6", "cg7", "gc", "gc1", "gc2", "gc3", "mv", "oA", "oB", "sP", "ev",
                     "bs", "bg", "bt", "bv", "bc"]
        sems = {n: stack.enter_context(nc.semaphore(n)) for n in sem_names}
        p = Prog(sems)

        bigA = psum("bigA", (128, 1024))
        bigB = psum("bigB", (128, 1024))
        ps47 = [psum(f"ps{i}", (128, 512)) for i in range(4, 8)]
        ps = [bigA[:, 0:512], bigA[:, 512:1024], bigB[:, 0:512], bigB[:, 512:1024]] + \
             [t[:, :] for t in ps47]
        sc_ps = [bigA[:, :], bigB[:, :]]

        ones_f = sb("ones_f", (1, 128), F32)
        ones128b = sb("ones128b", (128, 1), BF16)
        ind2b = sb("ind2b", (128, 2), BF16)
        ones16f = sb("ones16f", (16, 1), F32)
        ident_b = sb("ident_b", (128, 128), BF16)
        addmask_b = sb("addmask_b", (128, 4 * 512), BF16)
        ind2t_b = sb("ind2t_b", (2, 128), BF16)
        ones_bf = sb("ones_bf", (1, 128), BF16)
        gb2_b = sb("gb2_b", (2, E), BF16)
        ind2g_b = sb("ind2g_b", (2, E), BF16)

        p.emit("sync", lambda e: e.dma_start(out=ones16f[:, :], in_=din["ones16"]), inc="sIN", inc_by=16)
        p.emit("gpsimd", lambda e: e.dma_start(out=ind2b[:, :], in_=din["ind2"]), inc="sG", inc_by=16)
        p.emit("gpsimd", lambda e: e.dma_start(out=ident_b[:, :], in_=din["ident"]), inc="sG", inc_by=16)
        p.emit("gpsimd", lambda e: e.dma_start(out=addmask_b[:, :], in_=din["addmask"]), inc="sG", inc_by=16)
        p.emit("gpsimd", lambda e: e.dma_start(out=ind2t_b[:, :], in_=din["ind2t"]), inc="sG", inc_by=16)
        p.emit("gpsimd", lambda e: e.dma_start(out=gb2_b[:, :], in_=din["gb2"]), inc="sG", inc_by=16)
        p.emit("gpsimd", lambda e: e.dma_start(out=ind2g_b[:, :], in_=din["ind2g"]), inc="sG", inc_by=16)
        p.emit("vector", lambda e: e.memset(ones_f[:, :], 1.0), inc="sV")
        p.emit("vector", lambda e: e.memset(ones128b[:, :], 1.0), inc="sV")
        p.emit("vector", lambda e: e.memset(ones_bf[:, :], 1.0), inc="sV")


        rsc = stack.enter_context(contextlib.ExitStack())
        sbr = lambda name, shape, dt: rsc.enter_context(nc.sbuf_tensor(name, list(shape), dt))
        bias_f = sbr("bias_f", (128, ET), F32)
        bias_i = sbr("bias_i", (128, ET), F32)
        bias_c = sbr("bias_c", (128, ET), F32)
        incol = sbr("incol", (128, ET), F32)
        bo_row = sbr("bo_row", (1, E), F32)
        p.emit("sync", lambda e: e.dma_start(out=bias_f[:, :], in_=din["bf"]), inc="sIN", inc_by=16)
        p.emit("sync", lambda e: e.dma_start(out=bias_i[:, :], in_=din["bi"]), inc="sIN", inc_by=16)
        p.emit("sync", lambda e: e.dma_start(out=bias_c[:, :], in_=din["bc"]), inc="sIN", inc_by=16)
        p.emit("sync", lambda e: e.dma_start(out=incol[:, :], in_=din["in_c"]), inc="sIN", inc_by=16)
        p.emit("sync", lambda e: e.dma_start(out=bo_row[:, :], in_=din["bo"]), inc="sIN", inc_by=16)
        wslab = {w: sbr(f"w_{w}", (128, ET * E), BF16) for w in WNAMES}
        for w in WNAMES:
            dst = wslab[w]
            src_ = din[w].rearrange("(i p) m -> p i m", p=128)
            p.emit("gpsimd", lambda e, d=dst, s=src_: e.dma_start(
                out=d[:, :].rearrange("p (i m) -> p i m", i=ET), in_=s), inc="sG", inc_by=16)

        with contextlib.ExitStack() as rpre:
            sbp = lambda name, shape, dt: rpre.enter_context(nc.sbuf_tensor(name, list(shape), dt))
            xslab = [sbp(f"rp_xs{i}", (128, E * JPC // 128), F32) for i in range(2)]
            xbfs = [sbp(f"rp_xb{i}", (128, 2048), BF16) for i in range(2)]
            xsqs = [sbp(f"rp_xq{i}", (128, 2048), BF16) for i in range(2)]
            TH = T // 2
            mrows = sbp("rp_m", (1, TH * JPC), F32)
            qrows = sbp("rp_q", (1, TH * JPC), F32)
            trow = sbp("rp_t", (1, TH * JPC), F32)
            frow = [sbp(f"rp_f{i}", (1, 1024), F32) for i in range(2)]

            castv = {}
            sqv = {}
            statsv = {}
            foldv = {}
            pending_fold = None
            for t in range(T):
                th = t % TH
                par = t % 2
                rsem = "rA" if par == 0 else "rB"
                psm, psq = [(ps[4], ps[5]), (ps[6], ps[7]), (ps[0], ps[1]), (ps[2], ps[3])][t % 4]
                if t >= 2:
                    p.wait("sync", "sV", castv[t - 2])
                src = xjT[t].rearrange("(i p) j -> p i j", p=128)
                slabv = p.emit("sync", lambda e, s=src, d=xslab[par]: e.dma_start(
                    out=d[:, :].rearrange("p (i j) -> p i j", i=ET), in_=s), inc=rsem, inc_by=16)
                p.wait("vector", rsem, slabv)
                if t >= 2:
                    p.wait("vector", "sT", statsv[t - 2])
                p.drain("vector")
                p.emit("vector", lambda e, a=xbfs[par], b=xslab[par]: e.tensor_copy(a[:, :], b[:, :]), inc="sV")
                castv[t] = p.cnt["sV"]
                if pending_fold is not None:
                    pending_fold()
                    pending_fold = None
                p.wait("scalar", "sV", castv[t])
                if t >= 2:
                    p.wait("scalar", "sT", statsv[t - 2])
                p.emit("scalar", lambda e, a=xsqs[par], b=xbfs[par]: e.activation(
                    a[:, :], b[:, :], AF.Square), inc="sS")
                sqv[t] = p.cnt["sS"]
                p.wait("tensor", "sV", castv[t])
                if t >= 4:
                    p.wait("tensor", "sV", foldv[t - 4])
                for k in range(4):
                    p.emit("tensor", lambda e, k=k, b=xbfs[par], psm=psm: e.matmul(
                        psm[0:1, :], ones128b[:, :], b[:, 512 * k:512 * k + 512],
                        start=(k == 0), stop=(k == 3)))
                p.wait("tensor", "sS", sqv[t])
                for k in range(4):
                    p.emit("tensor", lambda e, k=k, b=xsqs[par], psq=psq: e.matmul(
                        psq[0:1, :], ones128b[:, :], b[:, 512 * k:512 * k + 512],
                        start=(k == 0), stop=(k == 3)),
                        inc=("sT" if k == 3 else None))
                statsv[t] = p.cnt["sT"]

                def emit_fold(t=t, psm=psm, psq=psq):
                    fpar = t % 2
                    p.wait("vector", "sT", statsv[t])
                    p.drain("vector")
                    p.emit("vector", lambda e, t=t, psm=psm, d=frow[fpar]: e.tensor_copy(d[:, 0:512], psm[0:1, :]))
                    p.emit("vector", lambda e, t=t, psq=psq, d=frow[fpar]: e.tensor_copy(d[:, 512:1024], psq[0:1, :]))
                    p.drain("vector")
                    th2 = t % TH
                    p.emit("vector", lambda e, th2=th2, d=frow[fpar]: e.tensor_tensor(
                        mrows[:, JPC * th2:JPC * (th2 + 1)], d[:, 0:JPC], d[:, JPC:2 * JPC], ALU.add))
                    p.emit("vector", lambda e, th2=th2, d=frow[fpar]: e.tensor_tensor(
                        qrows[:, JPC * th2:JPC * (th2 + 1)], d[:, 512:512 + JPC], d[:, 512 + JPC:512 + 2 * JPC], ALU.add), inc="sV")
                    foldv[t] = p.cnt["sV"]
                pending_fold = emit_fold
                if th == TH - 1:
                    pending_fold()
                    pending_fold = None
                    half = t // TH
                    p.drain("vector")
                    p.emit("vector", lambda e: e.tensor_scalar_mul(mrows[:, :], mrows[:, :], 1.0 / E))
                    p.drain("vector")
                    p.emit("vector", lambda e: e.tensor_tensor(trow[:, :], mrows[:, :], mrows[:, :], ALU.mult))
                    p.drain("vector")
                    p.emit("vector", lambda e: e.scalar_tensor_tensor(
                        qrows[:, :], qrows[:, :], 1.0 / E, trow[:, :], ALU.mult, ALU.subtract))
                    p.drain("vector")
                    p.emit("vector", lambda e: e.tensor_scalar_add(qrows[:, :], qrows[:, :], LN_EPS), inc="sV")
                    p.wait_cur("scalar", "sV")
                    p.emit("scalar", lambda e: e.activation(qrows[:, :], qrows[:, :], AF.Sqrt), inc="sS")
                    p.wait_cur("vector", "sS")
                    p.emit("vector", lambda e: e.reciprocal(qrows[:, :], qrows[:, :]))
                    p.drain("vector")
                    p.emit("vector", lambda e: e.tensor_tensor(trow[:, :], mrows[:, :], qrows[:, :], ALU.mult), inc="sV")
                    p.wait_cur("sync", "sV")
                    p.emit("sync", lambda e, half=half: e.dma_start(
                        out=stats_dram[0:1, TH * JPC * half:TH * JPC * (half + 1)], in_=qrows[:, :]), inc="sOUT", inc_by=16)
                    p.emit("sync", lambda e, half=half: e.dma_start(
                        out=stats_dram[1:2, TH * JPC * half:TH * JPC * (half + 1)], in_=trow[:, :]), inc="sOUT", inc_by=16)
                    if half == 0:
                        p.wait_cur("vector", "sOUT")
                    else:
                        p.wait_cur("sync", "sOUT")
            p.barrier("rpre-end")

        if True:
            uo_slab = sbr("w_uoT", (128, ET * E), BF16)
            p.emit("gpsimd", lambda e, d=uo_slab: e.dma_start(
                out=d[:, :].rearrange("p (i m) -> p i m", i=ET),
                in_=din["uoT"].rearrange("(i p) m -> p i m", p=128)), inc="sG", inc_by=16)

            cslab = [sbr(f"c_{i}", (128, ET * JPC), F32) for i in range(2)]
            cbf = [sbr(f"cb_{i}", (128, ET * JPC), BF16) for i in range(2)]
            xslab = [sbr(f"rm_xs{i}", (128, 2048), F32) for i in range(2)]
            xnsl = [sbr(f"rm_xn{i}", (128, 2048), BF16) for i in range(2)]
            strowb = [sbr(f"rm_st{i}", (128, 2 * JPC), F32) for i in range(2)]
            ntmp = [sbr(f"rm_nt{i}", (128, JPC), F32) for i in range(2)]
            gate_f = [sbr(f"g_f{i}", (128, JPC), F32) for i in range(2)]
            gate_i = [sbr(f"g_i{i}", (128, JPC), F32) for i in range(2)]
            gate_g = [sbr(f"g_g{i}", (128, JPC), F32) for i in range(2)]
            fc_t = [sbr(f"fc{i}", (128, JPC), F32) for i in range(2)]
            ig_t = [sbr(f"ig{i}", (128, JPC), F32) for i in range(2)]
            r_sb = sbr("r_sb", (128, 2 * E), F32)

            p.wait_cur("gpsimd", "sG")

            p.wait_cur("vector", "sIN")
            for m in range(ET):
                p.emit("vector", lambda e, m=m: e.tensor_copy(
                    cslab[0][:, JPC * m:JPC * (m + 1)],
                    incol[:, m:m + 1].broadcast_to([128, JPC])))
                p.emit("vector", lambda e, m=m: e.tensor_copy(
                    cbf[0][:, JPC * m:JPC * (m + 1)],
                    incol[:, m:m + 1].broadcast_to([128, JPC])),
                    inc=("sV" if m == ET - 1 else None))
            c_done_val = p.cnt["sV"]
            p.barrier("r-weights")

            evac_val = {}
            updv = {}
            xn_done = {}
            gdone_val = {}
            slab_need = {}

            def emit_slab_dma(t):
                par = t % 2
                rsem = "rA" if par == 0 else "rB"
                if t >= 2:
                    p.wait("sync", "sV", xn_done[t - 2])
                src = xjT[t].rearrange("(i p) j -> p i j", p=128)
                p.emit("sync", lambda e, s=src, d=xslab[par]: e.dma_start(
                    out=d[:, :].rearrange("p (i j) -> p i j", i=ET), in_=s), inc=rsem, inc_by=16)
                p.emit("sync", lambda e, t=t, d=strowb[par]: e.dma_start(
                    out=d[:, :].rearrange("p (r j) -> p r j", r=2),
                    in_=stats_dram[:, JPC * t:JPC * (t + 1)].unsqueeze(0).broadcast_to([128, 2, JPC])),
                    inc=rsem, inc_by=16)
                slab_need[t] = p.cnt[rsem]

            def emit_normalize(t):
                par = t % 2
                rsem = "rA" if par == 0 else "rB"
                sbv = strowb[par]
                p.wait("vector", rsem, slab_need[t])
                p.drain("vector")
                for i in range(ET):
                    tp = i % 2
                    if i >= 2:
                        p.drain("vector")
                    p.emit("vector", lambda e, i=i, tp=tp, xs=xslab[par], sbv=sbv: e.tensor_tensor(
                        ntmp[tp][:, :], xs[:, 256 * i:256 * i + 256], sbv[:, 0:JPC], ALU.mult))
                    p.drain("vector")
                    p.emit("vector", lambda e, i=i, tp=tp, xn=xnsl[par], sbv=sbv: e.tensor_tensor(
                        xn[:, 256 * i:256 * i + 256], ntmp[tp][:, :], sbv[:, JPC:2 * JPC], ALU.subtract),
                        inc=("sV" if i == ET - 1 else None))
                xn_done[t] = p.cnt["sV"]

            emit_slab_dma(0)
            emit_slab_dma(1)
            emit_normalize(0)

            for t in range(T):
                par = t % 2
                if t + 2 < T:
                    emit_slab_dma(t + 2)

                for m in range(ET):
                    mset = m % 2
                    f_ps, i_ps, g_ps = (ps[0], ps[1], ps[2]) if mset == 0 else (ps[3], ps[5], ps[6])
                    if (t, m - 2) in evac_val:
                        p.wait("tensor", "sS", evac_val[(t, m - 2)])
                    elif (t - 1, m + 6) in evac_val:
                        p.wait("tensor", "sS", evac_val[(t - 1, m + 6)])
                    if m == 0:
                        p.wait("tensor", "sV", xn_done[t])
                    for wn, pp in [("wfT", f_ps), ("wiT", i_ps), ("wcT", g_ps)]:
                        for i in range(ET):
                            p.emit("tensor", lambda e, wn=wn, pp=pp, i=i, m=m, xn=xnsl[par]: e.matmul(
                                pp[:, 0:256],
                                wslab[wn][:, E * i + 128 * m:E * i + 128 * m + 128],
                                xn[:, 256 * i:256 * i + 256],
                                start=(i == 0), stop=False))
                    if m == 0:
                        p.wait("tensor", "sV", c_done_val)
                    cb = cbf[t % 2]
                    for gi, (un, pp) in enumerate([("ufT", f_ps), ("uiT", i_ps), ("ucT", g_ps)]):
                        for i in range(ET):
                            last = (gi == 2 and i == ET - 1)
                            p.emit("tensor", lambda e, un=un, pp=pp, i=i, m=m, cb=cb: e.matmul(
                                pp[:, 0:256],
                                wslab[un][:, E * i + 128 * m:E * i + 128 * m + 128],
                                cb[:, 256 * i:256 * i + 256],
                                start=False, stop=(i == ET - 1)),
                                inc=("sT" if last else None))
                    gdone_val[(t, m)] = p.cnt["sT"]

                    p.wait("scalar", "sT", gdone_val[(t, m)])
                    if (t, m - 2) in updv:
                        p.wait("scalar", "sV", updv[(t, m - 2)])
                    elif (t - 1, m + 6) in updv:
                        p.wait("scalar", "sV", updv[(t - 1, m + 6)])
                    p.emit("scalar", lambda e, m=m, fp=f_ps, d=gate_f[mset]: e.activation(
                        d[:, :], fp[:, 0:256], AF.Sigmoid, bias=bias_f[:, m:m + 1]))
                    p.emit("scalar", lambda e, m=m, ip=i_ps, d=gate_i[mset]: e.activation(
                        d[:, :], ip[:, 0:256], AF.Sigmoid, bias=bias_i[:, m:m + 1]))
                    p.emit("scalar", lambda e, m=m, gp=g_ps, d=gate_g[mset]: e.activation(
                        d[:, :], gp[:, 0:256], AF.Tanh, bias=bias_c[:, m:m + 1]), inc="sS")
                    evac_val[(t, m)] = p.cnt["sS"]

                    cin = cslab[t % 2]
                    cout = cslab[(t + 1) % 2]
                    cbout = cbf[(t + 1) % 2]
                    p.wait("vector", "sS", evac_val[(t, m)])
                    p.drain("vector")
                    p.emit("vector", lambda e, m=m, ms=mset, ci=cin: e.tensor_tensor(
                        fc_t[ms][:, :], gate_f[ms][:, :], ci[:, JPC * m:JPC * (m + 1)], ALU.mult))
                    p.emit("vector", lambda e, ms=mset: e.tensor_tensor(
                        ig_t[ms][:, :], gate_i[ms][:, :], gate_g[ms][:, :], ALU.mult))
                    p.drain("vector")
                    p.emit("vector", lambda e, m=m, ms=mset, co=cout: e.tensor_tensor(
                        co[:, JPC * m:JPC * (m + 1)], fc_t[ms][:, :], ig_t[ms][:, :], ALU.add))
                    p.emit("vector", lambda e, m=m, ms=mset, cb2=cbout: e.tensor_tensor(
                        cb2[:, JPC * m:JPC * (m + 1)], fc_t[ms][:, :], ig_t[ms][:, :], ALU.add), inc="sV")
                    updv[(t, m)] = p.cnt["sV"]

                    if m == 1 and t + 1 < T:
                        emit_normalize(t + 1)
                c_done_val = p.cnt["sV"]

            cfin = cbf[T % 2]
            p.wait("tensor", "sV", c_done_val)
            p.wait_cur("tensor", "sS")
            for jh in range(2):
                for oc in range(2):
                    pp = ps[4 + 2 * jh + oc]
                    for i in range(ET):
                        p.emit("tensor", lambda e, jh=jh, oc=oc, i=i, pp=pp, cf=cfin: e.matmul(
                            pp[:, :],
                            cf[:, JPC * i + 128 * jh:JPC * i + 128 * jh + 128],
                            uo_slab[:, E * i + 512 * oc:E * i + 512 * oc + 512],
                            start=(i == 0), stop=False))
                    p.emit("tensor", lambda e, oc=oc, pp=pp: e.matmul(
                        pp[:, :], ones_f[:, :], bo_row[:, 512 * oc:512 * oc + 512],
                        start=False, stop=True), inc="sT")
            rdone = p.cnt["sT"]
            p.wait("vector", "sT", rdone)
            for jh in range(2):
                for oc in range(2):
                    pp = ps[4 + 2 * jh + oc]
                    p.emit("vector", lambda e, jh=jh, oc=oc, pp=pp: e.tensor_copy(
                        r_sb[:, E * jh + 512 * oc:E * jh + 512 * oc + 512], pp[:, :]), inc="sV")
            p.wait_cur("gpsimd", "sV")
            if mode == "r":
                p.emit("gpsimd", lambda e: e.dma_start(
                    out=out_r.rearrange("(jh p) e -> p jh e", p=128),
                    in_=r_sb[:, :].rearrange("p (jh e) -> p jh e", jh=2)), inc="sOUT", inc_by=16)
                p.wait_cur("gpsimd", "sOUT")
                p.barrier("end")
            else:
                p.emit("gpsimd", lambda e: e.dma_start(
                    out=cc_in.rearrange("(jh p) e -> p jh e", p=128),
                    in_=r_sb[:, :].rearrange("p (jh e) -> p jh e", jh=2)), inc="sG", inc_by=16)
                p.wait_cur("gpsimd", "sG")
                p.emit("gpsimd", lambda e: e.collective_compute(
                    "AllGather", ALU.bypass,
                    replica_groups=[list(range(NC))],
                    ins=[cc_in], outs=[cc_out]), inc="sCC")
                p.barrier("r-end")
        rsc.close()

        if mode == "r":
            _finalize(nc, p)
            return nc

        env = dict(sb=sb, ones_f=ones_f, ones128b=ones128b, ind2b=ind2b,
                   ones16f=ones16f, ident_b=ident_b,
                   addmask_b=addmask_b, ind2t_b=ind2t_b, ones_bf=ones_bf,
                   gb2_b=gb2_b, ind2g_b=ind2g_b,
                   bo_row=bo_row, sc_ps=sc_ps,
                   xb=xb, xbT=xbT, out=out, cc_out=cc_out)
        _emit_phase_a(nc, p, ps, din, env)
        _finalize(nc, p)
        return nc


def _emit_phase_a(nc, p, ps, din, env):
    sb = env["sb"]
    ones_f = env["ones_f"]
    ones128b = env["ones128b"]
    ind2b = env["ind2b"]
    ones16f = env["ones16f"]
    ident_b = env["ident_b"]
    addmask_b = env["addmask_b"]
    ind2t_b = env["ind2t_b"]
    ones_bf = env["ones_bf"]
    gb2_b = env["gb2_b"]
    ind2g_b = env["ind2g_b"]
    sc_ps = env["sc_ps"]
    xbd = env["xb"]
    xbTd = env["xbT"]
    outd = env["out"]
    cc_out = env["cc_out"]

    xnT = sb("a_xnT", (128, ET * S), BF16)
    qkvT = sb("a_qkvT", (128, NP * S), BF16)
    neg20c = sb("a_neg20", (128, 1), F32)
    p.emit("vector", lambda e: e.memset(neg20c[:, :], -EXP_SHIFT), inc="sV")

    for b in range(BPC):
        a12x = contextlib.ExitStack()
        sba12 = lambda n, s_, d: a12x.enter_context(nc.sbuf_tensor(n, list(s_), d))
        c16 = sba12(f"a_c16{b}", (16, S), F32)
        s1b = sba12(f"a_s1b{b}", (16, S), F32)
        s2b = sba12(f"a_s2b{b}", (16, S), F32)
        stg_d = [[sba12(f"a_stgd{b}_{i}_{j}", (1, 512), F32) for j in range(2)] for i in range(2)]
        stgd_sem = [["cg", "cg6"], ["cg1", "cg7"]]
        stgd_free2 = [[p.cnt["cg"], p.cnt["cg6"]], [p.cnt["cg1"], p.cnt["cg7"]]]
        sqt1 = sba12(f"a2_sq{b}", (128, S), BF16)
        sqt = [sqt1, sqt1]
        stg_s1 = [sba12(f"a2_ss1{b}_{i}", (2, 512), F32) for i in range(2)]
        stg_s2 = [sba12(f"a2_ss2{b}_{i}", (2, 512), F32) for i in range(2)]
        stg_free = [(p.cnt["cg2"], p.cnt["cg4"]), (p.cnt["cg3"], p.cnt["cg5"])]
        vsc = contextlib.ExitStack()
        vext = vsc.enter_context(nc.sbuf_tensor(f"a_vext{b}", [128, STL * H * 65], BF16))
        def emit_vproj_mms(ch, k0, k1):
            p.wait("tensor", "sG", wv_ready)
            p.wait("tensor", "sV", max(vo_ready, normv0[ch]))
            for k in range(k0, k1):
                sg = 4 * ch + k % 4
                ec = k // 4
                pp = ps[6 + k % 2]
                if (ch, k - 2) in vpev:
                    p.wait("tensor", "ev", vpev[(ch, k - 2)])
                elif (ch - 1, k + 6) in vpev:
                    p.wait("tensor", "ev", vpev[(ch - 1, k + 6)])
                for i in range(ET):
                    p.emit("tensor", lambda e, sg=sg, ec=ec, i=i, pp=pp, xnT=xnT, w=wvsh: e.matmul(
                        pp[:, :],
                        xnT[:, 2048 * i + 128 * sg:2048 * i + 128 * sg + 128],
                        w[:, E * i + 512 * ec:E * i + 512 * ec + 512],
                        start=(i == 0), stop=(i == ET - 1)),
                        inc=("sT" if i == ET - 1 else None))
                vpsT[(ch, k)] = p.cnt["sT"]
                evplan[0] += 1
                vpev[(ch, k)] = evplan[0]

        def emit_vproj_evacs(ch, k0, k1, vext=vext):
            p.wait("scalar", "sV", vo_ready)
            for k in range(k0, k1):
                sg = 4 * ch + k % 4
                ec = k // 4
                pp = ps[6 + k % 2]
                p.wait("scalar", "sT", vpsT[(ch, k)])
                p.emit("scalar", lambda e, sg=sg, ec=ec, pp=pp, vext=vext: e.activation(
                    vext[:, 1040 * sg + 65 * 8 * ec:1040 * sg + 65 * 8 * ec + 520]
                        .rearrange("p (h d) -> p h d", d=65)[:, :, 0:64],
                    pp[:, :].rearrange("p (h d) -> p h d", d=64), AF.Copy), inc="ev")
                assert p.cnt["ev"] == vpev[(ch, k)]

        with contextlib.ExitStack() as a0:
            sba = lambda n, s_, d: a0.enter_context(nc.sbuf_tensor(n, list(s_), d))
            xb1 = sba(f"a0_xbf{b}", (128, ET * 512), BF16)
            xb2 = sba(f"a0_xbf2{b}", (128, ET * 512), BF16)
            xq1 = sba(f"a0_xsq{b}", (128, ET * 512), BF16)
            xbfs = [xb1, xb2]
            xsqs = [xq1, xq1]
            wvsh = sba(f"a_wvsh{b}", (128, ET * E), BF16)
            mrow = sba(f"a0_m{b}", (1, 512), F32)
            erow = sba(f"a0_e{b}", (1, 512), F32)
            trow = sba(f"a0_t{b}", (1, 512), F32)
            nt1 = sba(f"a0_nt{b}", (128, 512), F32)
            ntmp = [nt1, nt1]
            browbf = sba(f"a0_bb{b}", (1, 1024), BF16)
            p.emit("gpsimd", lambda e, wvsh=wvsh: e.dma_start(
                out=wvsh[:, :].rearrange("p (i m) -> p i m", i=ET),
                in_=din["WvTp"].rearrange("(i p) m -> p i m", p=128)), inc="sG", inc_by=16)
            p.drain("vector")
            p.emit("vector", lambda e, vext=vext: e.memset(vext[:, 64:STL * H * 65:65], 1.0), inc="sV")
            wv_ready = p.cnt["sG"]
            vo_ready = p.cnt["sV"]
            castv0 = {}
            sq0v = {}
            statv0 = {}
            normv0 = {}
            rowev = {}
            vpev = {}
            vpsT = {}
            evplan = [p.cnt["ev"]]
            for ch in range(NCH):
                par = ch % 2
                xsem = "aXB" if par == 0 else "aXB2"
                psm, psq = ps[0], ps[1]
                psi, psv = (ps[2], ps[3]) if par == 0 else (ps[4], ps[5])
                if ch >= 2:
                    p.wait("gpsimd", "sV", normv0[ch - 2])
                src2 = xbTd[b][:, 512 * ch:512 * ch + 512].rearrange("(i p) t -> p i t", p=128)
                chv = p.emit("gpsimd", lambda e, s=src2, d=xbfs[par]: e.dma_start(
                    out=d[:, :].rearrange("p (i t) -> p i t", i=ET), in_=s), inc=xsem, inc_by=16)
                p.wait("scalar", xsem, chv)
                if ch >= 1:
                    p.wait("scalar", "sT", statv0[ch - 1])
                p.emit("scalar", lambda e, d=xsqs[par], s=xbfs[par]: e.activation(
                    d[:, :], s[:, :], AF.Square), inc="sS")
                sq0v[ch] = p.cnt["sS"]
                p.wait("tensor", xsem, chv)
                if ch >= 1:
                    p.wait("tensor", "sV", rowev[ch - 1])
                for i in range(ET):
                    p.emit("tensor", lambda e, i=i, s=xbfs[par], psm=psm: e.matmul(
                        psm[0:1, :], ones128b[:, :], s[:, 512 * i:512 * i + 512],
                        start=(i == 0), stop=(i == ET - 1)))
                p.wait("tensor", "sS", sq0v[ch])
                for i in range(ET):
                    p.emit("tensor", lambda e, i=i, s=xsqs[par], psq=psq: e.matmul(
                        psq[0:1, :], ones128b[:, :], s[:, 512 * i:512 * i + 512],
                        start=(i == 0), stop=(i == ET - 1)), inc="sT")
                statv0[ch] = p.cnt["sT"]
                p.wait("vector", "sT", statv0[ch])
                p.drain("vector")
                p.emit("vector", lambda e, mrow=mrow, psm=psm: e.tensor_scalar_mul(mrow[:, :], psm[0:1, :], 1.0 / E))
                p.emit("vector", lambda e, erow=erow, psq=psq: e.tensor_scalar_mul(erow[:, :], psq[0:1, :], 1.0 / E), inc="sV")
                rowev[ch] = p.cnt["sV"]
                p.drain("vector")
                p.emit("vector", lambda e, trow=trow, mrow=mrow: e.tensor_tensor(trow[:, :], mrow[:, :], mrow[:, :], ALU.mult))
                p.drain("vector")
                p.emit("vector", lambda e, erow=erow, trow=trow: e.tensor_tensor(erow[:, :], erow[:, :], trow[:, :], ALU.subtract))
                p.drain("vector")
                p.emit("vector", lambda e, erow=erow: e.tensor_scalar_add(erow[:, :], erow[:, :], LN_EPS), inc="sV")
                p.wait_cur("scalar", "sV")
                p.emit("scalar", lambda e, erow=erow: e.activation(erow[:, :], erow[:, :], AF.Sqrt), inc="sS")
                sqrt0v = p.cnt["sS"]
                if ch >= 1:
                    emit_vproj_mms(ch - 1, 0, 8)
                    emit_vproj_evacs(ch - 1, 0, 8)
                p.wait("vector", "sS", sqrt0v)
                p.emit("vector", lambda e, erow=erow: e.reciprocal(erow[:, :], erow[:, :]))
                p.drain("vector")
                p.emit("vector", lambda e, trow=trow, mrow=mrow, erow=erow: e.tensor_tensor(trow[:, :], mrow[:, :], erow[:, :], ALU.mult))
                p.drain("vector")
                p.emit("vector", lambda e, erow=erow, browbf=browbf: e.tensor_copy(browbf[:, 0:512], erow[:, :]))
                p.emit("vector", lambda e, trow=trow, browbf=browbf: e.tensor_copy(browbf[:, 512:1024], trow[:, :]), inc="sV")
                p.wait_cur("tensor", "sV")
                p.emit("tensor", lambda e, browbf=browbf, psi=psi: e.matmul(psi[:, :], ones_bf[:, :], browbf[:, 0:512], start=True, stop=True))
                p.emit("tensor", lambda e, browbf=browbf, psv=psv: e.matmul(psv[:, :], ones_bf[:, :], browbf[:, 512:1024], start=True, stop=True), inc="sT")
                p.wait_cur("vector", "sT")
                for i in range(ET):
                    ip = i % 2
                    p.drain("vector")
                    p.emit("vector", lambda e, i=i, d=ntmp[ip], s=xbfs[par], psi=psi: e.tensor_tensor(
                        d[:, :], s[:, 512 * i:512 * i + 512], psi[:, :], ALU.mult))
                    p.drain("vector")
                    p.emit("vector", lambda e, i=i, s=ntmp[ip], ch=ch, xnT=xnT, psv=psv: e.tensor_tensor(
                        xnT[:, 2048 * i + 512 * ch:2048 * i + 512 * ch + 512],
                        s[:, :], psv[:, :], ALU.subtract),
                        inc=("sV" if i == ET - 1 else None))
                normv0[ch] = p.cnt["sV"]
            emit_vproj_mms(NCH - 1, 0, 8)
            emit_vproj_evacs(NCH - 1, 0, 8)
            p.barrier(f"a0-{b}")

        with contextlib.ExitStack() as a12:
            with contextlib.ExitStack() as a1:
                sba1 = lambda n, s_, d: a1.enter_context(nc.sbuf_tensor(n, list(s_), d))
                qt1 = sba1(f"a_qt{b}", (128, S), BF16)
                kt1 = sba1(f"a_kt{b}", (128, S), BF16)
                qt = [qt1, qt1]
                kt = [kt1, kt1]
                ptil = sba1(f"a_ptil{b}", (128, 6 * 1024), BF16)
                wq1 = sba1(f"a_wq{b}", (128, ET * 128), BF16)
                wk1 = sba1(f"a_wk{b}", (128, ET * 128), BF16)
                wq_sl = [wq1, wq1]
                wk_sl = [wk1, wk1]


                def emit_wslice(pair):
                    par = pair % 2
                    wsem = "wA" if par == 0 else "wB"
                    if pair >= 1:
                        p.wait("gpsimd", "sT", qkproj_done[pair - 1])
                    for wn, dst in [("WqT", wq_sl[par]), ("WkT", wk_sl[par])]:
                        src = din[wn][:, 128 * pair:128 * pair + 128].rearrange(
                            "(i p) c -> p i c", p=128)
                        p.emit("gpsimd", lambda e, s=src, d=dst: e.dma_start(
                            out=d[:, :].rearrange("p (i c) -> p i c", i=ET), in_=s),
                            inc=wsem, inc_by=16)
                    wsl_need[pair] = p.cnt[wsem]

                qkproj_done = {}
                qkevac_done = {}
                wsl_need = {}
                expv = {}
                exph = {}
                maskv = {}
                exp_guard = {(0, 0): 0, (0, 1): 0, (1, 0): 0, (1, 1): 0}
                av_guard = [0, 0]
                emit_wslice(0)
                def make_stats_pieces(spair):
                    sq_done = p.cnt["sV"]
                    state = {"evac": 0}

                    def piece(ch, spair=spair):
                        cpar = ch % 2
                        s1sem = "cg2" if cpar == 0 else "cg3"
                        s2sem = "cg4" if cpar == 0 else "cg5"
                        p.wait("vector", s1sem, stg_free[cpar][0])
                        p.wait("vector", s2sem, stg_free[cpar][1])
                        p.wait("tensor", "sV", max(sq_done, state["evac"]))
                        p.emit("tensor", lambda e, spair=spair, ch=ch, qkvT=qkvT: e.matmul(
                            ps[6][0:2, :], ind2b[:, :],
                            qkvT[:, 2048 * spair + 512 * ch:2048 * spair + 512 * ch + 512],
                            start=True, stop=True))
                        p.emit("tensor", lambda e, ch=ch, s=sqt[spair % 2]: e.matmul(
                            ps[7][0:2, :], ind2b[:, :],
                            s[:, 512 * ch:512 * ch + 512],
                            start=True, stop=True), inc="sT")
                        p.wait_cur("vector", "sT")
                        p.drain("vector")
                        p.emit("vector", lambda e, d=stg_s1[cpar]: e.tensor_copy(d[:, :], ps[6][0:2, :]))
                        p.emit("vector", lambda e, d=stg_s2[cpar]: e.tensor_copy(d[:, :], ps[7][0:2, :]), inc="sV")
                        state["evac"] = p.cnt["sV"]
                        p.wait_cur("sync", "sV")
                        p.emit("sync", lambda e, spair=spair, ch=ch, d=s1b, s=stg_s1[cpar]: e.dma_start(
                            out=d[2 * spair:2 * spair + 2, 512 * ch:512 * ch + 512], in_=s[:, :]),
                            inc=s1sem, inc_by=16)
                        p.emit("sync", lambda e, spair=spair, ch=ch, d=s2b, s=stg_s2[cpar]: e.dma_start(
                            out=d[2 * spair:2 * spair + 2, 512 * ch:512 * ch + 512], in_=s[:, :]),
                            inc=s2sem, inc_by=16)
                        stg_free[cpar] = (p.cnt[s1sem], p.cnt[s2sem])
                    return piece

                pending_piece = None
                for pair in range(NP):
                    par = pair % 2
                    wsem = "wA" if par == 0 else "wB"
                    p.wait("tensor", wsem, wsl_need[pair])
                    if pair >= 1:
                        p.wait("tensor", "sV", qkevac_done[pair - 1])
                    p.wait("tensor", "sS", max(exp_guard.values()))
                    projevac = {}
                    for c in range(NCH):
                        pq = ps[c % 2]
                        pk = ps[2 + c % 2]
                        if c >= 2:
                            p.wait("tensor", "sV", projevac[c - 2])
                        for i in range(ET):
                            p.emit("tensor", lambda e, c=c, i=i, pq=pq, w=wq_sl[par]: e.matmul(
                                pq[:, :], w[:, 128 * i:128 * i + 128],
                                xnT[:, 2048 * i + 512 * c:2048 * i + 512 * c + 512],
                                start=(i == 0), stop=(i == ET - 1)),
                                inc=("sT" if i == ET - 1 else None))
                        qv = p.cnt["sT"]
                        for i in range(ET):
                            p.emit("tensor", lambda e, c=c, i=i, pk=pk, w=wk_sl[par]: e.matmul(
                                pk[:, :], w[:, 128 * i:128 * i + 128],
                                xnT[:, 2048 * i + 512 * c:2048 * i + 512 * c + 512],
                                start=(i == 0), stop=(i == ET - 1)),
                                inc=("sT" if i == ET - 1 else None))
                        kv = p.cnt["sT"]
                        if pending_piece is not None:
                            pending_piece(c)
                        p.wait("vector", "sT", qv)
                        p.emit("vector", lambda e, c=c, pq=pq, d=qt[par]: e.tensor_copy(
                            d[:, 512 * c:512 * c + 512], pq[:, :]))
                        p.wait("vector", "sT", kv)
                        p.emit("vector", lambda e, c=c, pk=pk, d=kt[par]: e.tensor_scalar_mul(
                            d[:, 512 * c:512 * c + 512], pk[:, :], 1.0 / (D ** 0.5)), inc="sV")
                        projevac[c] = p.cnt["sV"]
                    qkproj_done[pair] = p.cnt["sT"]
                    qkevac_done[pair] = p.cnt["sV"]
                    if pair + 1 < NP:
                        emit_wslice(pair + 1)

                    for c in range(NCH):
                        nst = 4 * c + 4

                        def emit_scores(sg, c=c):
                            spp = sc_ps[sg % 2]
                            diag = sg >= 4 * c
                            j = sg - 4 * c
                            hv = []
                            for hh in range(2):
                                p.wait("tensor", "sS", exp_guard[(sg % 2, hh)])
                                if sg < 2 and c == 0 and hh == 0:
                                    p.wait("tensor", "sV", qkevac_done[pair])
                                p.emit("tensor", lambda e, sg=sg, c=c, hh=hh, spp=spp, k=kt[par], q=qt[par], diag=diag: e.matmul(
                                    spp[:, 512 * hh:512 * hh + 512],
                                    k[64 * hh:64 * hh + 64, 128 * sg:128 * sg + 128],
                                    q[64 * hh:64 * hh + 64, 512 * c:512 * c + 512],
                                    start=True, stop=not diag),
                                    inc=(None if diag else "sT"))
                                if diag:
                                    p.emit("tensor", lambda e, hh=hh, j=j, spp=spp: e.matmul(
                                        spp[:, 512 * hh:512 * hh + 512],
                                        ident_b[:, :], addmask_b[:, 512 * j:512 * j + 512],
                                        start=False, stop=True),
                                        inc="sT")
                                hv.append(p.cnt["sT"])
                            slot = sg % 6
                            for hh in range(2):
                                p.wait("scalar", "sT", hv[hh])
                                p.emit("scalar", lambda e, slot=slot, hh=hh, spp=spp, ptil=ptil: e.activation(
                                    ptil[:, 1024 * slot + 512 * hh:1024 * slot + 512 * hh + 512],
                                    spp[:, 512 * hh:512 * hh + 512],
                                    AF.Exp, bias=neg20c[:, 0:1]), inc="sS")
                                exph[(pair, c, sg, hh)] = p.cnt["sS"]
                                exp_guard[(sg % 2, hh)] = p.cnt["sS"]
                            expv[(pair, c, sg)] = p.cnt["sS"]

                        def emit_av(sg, c=c, nst=nst):
                            slot = sg % 6
                            if sg == 0 and (pair, c) != (0, 0):
                                p.wait_cur("tensor", "sV")
                            for hh in range(2):
                                p.wait("tensor", "sS", exph[(pair, c, sg, hh)])
                                p.emit("tensor", lambda e, hh=hh, sg=sg, slot=slot, pair=pair, vext=vext, ptil=ptil,
                                    st_=(sg == 0), sp_=(sg == nst - 1): e.matmul(
                                    ps[4 + hh][0:65, :],
                                    vext[:, 1040 * sg + 65 * (2 * pair + hh):1040 * sg + 65 * (2 * pair + hh) + 65],
                                    ptil[:, 1024 * slot + 512 * hh:1024 * slot + 512 * hh + 512],
                                    start=st_, stop=sp_),
                                    inc=("sT" if (hh == 1 and sg == nst - 1) else None))

                        for sg in range(nst):
                            emit_scores(sg)
                            if sg >= 3:
                                emit_av(sg - 3)
                        for tail in range(max(0, nst - 3), nst):
                            emit_av(tail)
                        avdone = p.cnt["sT"]
                        p.wait("vector", "sT", avdone)
                        for hh in range(2):
                            p.emit("vector", lambda e, hh=hh, c=c, pair=pair, qkvT=qkvT: e.tensor_copy(
                                qkvT[64 * hh:64 * hh + 64, 2048 * pair + 512 * c:2048 * pair + 512 * c + 512],
                                ps[4 + hh][0:64, :]), inc="sV")
                        cp = c % 2
                        for hh in range(2):
                            p.wait("vector", stgd_sem[hh][cp], stgd_free2[hh][cp])
                            p.emit("vector", lambda e, hh=hh, cp=cp, d=stg_d[hh][cp]: e.tensor_copy(
                                d[:, :], ps[4 + hh][64:65, :]), inc="sV")
                        p.wait_cur("sync", "sV")
                        for hh in range(2):
                            p.emit("sync", lambda e, hh=hh, c=c, pair=pair, c16=c16, s=stg_d[hh][cp]: e.dma_start(
                                out=c16[2 * pair + hh:2 * pair + hh + 1, 512 * c:512 * c + 512], in_=s[:, :]),
                                inc=stgd_sem[hh][cp], inc_by=16)
                            stgd_free2[hh][cp] = p.cnt[stgd_sem[hh][cp]]

                    pp2 = pair % 2
                    if pair >= 1:
                        p.wait_cur("vector", "sT")
                    p.drain("vector")
                    p.emit("vector", lambda e, pair=pair, d=sqt[pp2], qkvT=qkvT: e.tensor_tensor(
                        d[:, :], qkvT[:, 2048 * pair:2048 * pair + 2048],
                        qkvT[:, 2048 * pair:2048 * pair + 2048], ALU.mult), inc="sV")
                    pending_piece = make_stats_pieces(pair)
                for ch in range(NCH):
                    pending_piece(ch)
                p.barrier(f"a1-{b}")
            vsc.close()
            w3 = contextlib.ExitStack()
            wosh = w3.enter_context(nc.sbuf_tensor(f"a3_wosh{b}", [128, ET * E], BF16))
            wwsh = w3.enter_context(nc.sbuf_tensor(f"a3_wwsh{b}", [128, ET * E], BF16))
            p.emit("gpsimd", lambda e, wosh=wosh: e.dma_start(
                out=wosh[:, :].rearrange("p (i m) -> p i m", i=ET),
                in_=din["WoT"].rearrange("(i p) m -> p i m", p=128)), inc="sG", inc_by=16)
            p.emit("gpsimd", lambda e, wwsh=wwsh: e.dma_start(
                out=wwsh[:, :].rearrange("p (i m) -> p i m", i=ET),
                in_=din["woT"].rearrange("(i p) m -> p i m", p=128)), inc="sG", inc_by=16)

            with contextlib.ExitStack() as a2:
                sba2 = lambda n, s_, d: a2.enter_context(nc.sbuf_tensor(n, list(s_), d))
                scstbf2 = [sba2(f"a2_scbf2{b}_{i}", (2, S), BF16) for i in range(4)]
                minv2 = sba2(f"a2_minv2{b}", (2, S), BF16)
                scaler = sba2(f"a2_scaler{b}", (16, S), F32)
                erowbf2 = sba2(f"a2_ebf{b}", (1, S), BF16)
                minvbf = sba2(f"a2_mbf{b}", (1, S), BF16)
                mrow2 = s1b[0:1, :]
                erow2 = s2b[0:1, :]
                trow2 = scaler[0:1, :]
                ntmp = [sba2(f"a2_nt{b}_{i}", (128, 512), F32) for i in range(2)]

                p.wait_cur("vector", "cg")
                p.wait_cur("vector", "cg1")
                p.wait_cur("vector", "cg2")
                p.wait_cur("vector", "cg3")
                p.wait_cur("vector", "cg4")
                p.wait_cur("vector", "cg5")
                p.wait_cur("vector", "cg6")
                p.wait_cur("vector", "cg7")
                p.drain("vector")
                p.emit("vector", lambda e, c16=c16: e.reciprocal(c16[:, :], c16[:, :]))
                p.drain("vector")
                p.emit("vector", lambda e, s1b=s1b, c16=c16: e.tensor_tensor(s1b[:, :], s1b[:, :], c16[:, :], ALU.mult))
                p.drain("vector")
                p.emit("vector", lambda e, s2b=s2b, c16=c16: e.tensor_tensor(s2b[:, :], s2b[:, :], c16[:, :], ALU.mult))
                p.drain("vector")
                p.emit("vector", lambda e, s2b=s2b, c16=c16: e.tensor_tensor(s2b[:, :], s2b[:, :], c16[:, :], ALU.mult), inc="sV")

                p.wait_cur("tensor", "sV")
                for ch in range(NCH):
                    p.emit("tensor", lambda e, ch=ch, s=s1b: e.matmul(
                        ps[ch][0:1, :], ones16f[:, :], s[:, 512 * ch:512 * ch + 512],
                        start=True, stop=True))
                    p.emit("tensor", lambda e, ch=ch, s=s2b: e.matmul(
                        ps[4 + ch][0:1, :], ones16f[:, :], s[:, 512 * ch:512 * ch + 512],
                        start=True, stop=True), inc="sT")
                p.wait_cur("vector", "sT")
                p.drain("vector")
                for ch in range(NCH):
                    p.emit("vector", lambda e, ch=ch, mrow2=mrow2: e.tensor_scalar_mul(
                        mrow2[:, 512 * ch:512 * ch + 512], ps[ch][0:1, :], 1.0 / E))
                    p.emit("vector", lambda e, ch=ch, erow2=erow2: e.tensor_scalar_mul(
                        erow2[:, 512 * ch:512 * ch + 512], ps[4 + ch][0:1, :], 1.0 / E), inc="sV")
                p.drain("vector")
                p.emit("vector", lambda e, trow2=trow2, mrow2=mrow2: e.tensor_tensor(trow2[:, :], mrow2[:, :], mrow2[:, :], ALU.mult))
                p.drain("vector")
                p.emit("vector", lambda e, erow2=erow2, trow2=trow2: e.tensor_tensor(erow2[:, :], erow2[:, :], trow2[:, :], ALU.subtract))
                p.drain("vector")
                p.emit("vector", lambda e, erow2=erow2: e.tensor_scalar_add(erow2[:, :], erow2[:, :], LN_EPS), inc="sV")
                p.wait_cur("scalar", "sV")
                p.emit("scalar", lambda e, erow2=erow2: e.activation(erow2[:, :], erow2[:, :], AF.Sqrt), inc="sS")
                p.wait_cur("vector", "sS")
                p.emit("vector", lambda e, erow2=erow2: e.reciprocal(erow2[:, :], erow2[:, :]))
                p.drain("vector")
                p.emit("vector", lambda e, minvbf=minvbf, mrow2=mrow2, erow2=erow2: e.tensor_tensor(
                    minvbf[:, :], mrow2[:, :], erow2[:, :], ALU.mult))
                p.drain("vector")
                p.emit("vector", lambda e, erow2=erow2, erowbf2=erowbf2: e.tensor_copy(erowbf2[:, :], erow2[:, :]), inc="sV")
                p.wait_cur("tensor", "sV")
                for ch in range(NCH):
                    p.emit("tensor", lambda e, ch=ch, erowbf2=erowbf2: e.matmul(
                        ps[ch][0:16, :], ones_bf[0:1, 0:16], erowbf2[:, 512 * ch:512 * ch + 512],
                        start=True, stop=True), inc="sT")
                p.wait_cur("vector", "sT")
                for ch in range(NCH):
                    p.emit("vector", lambda e, ch=ch, scaler=scaler, c16=c16: e.tensor_tensor(
                        scaler[:, 512 * ch:512 * ch + 512], c16[:, 512 * ch:512 * ch + 512],
                        ps[ch][0:16, :], ALU.mult), inc="sV")
                scalv = p.cnt["sV"]

                p.wait_cur("sync", "sV")
                p.emit("sync", lambda e, d=minv2, s=minvbf: e.dma_start(out=d[0:1, :], in_=s[:, :]), inc="mv", inc_by=16)
                p.wait_cur("gpsimd", "sV")
                p.emit("gpsimd", lambda e, d=minv2: e.dma_start(out=d[1:2, :], in_=din["negrow"]), inc="sG", inc_by=16)
                p.wait_cur("gpsimd", "sG")
                mv2 = p.cnt["mv"]
                normv2 = {}
                scv = {}
                scbv = {}
                for pair in range(NP):
                    psl = pair % 4
                    csem = ["gc", "gc1", "gc2", "gc3"][psl]
                    p.wait("gpsimd", "sV", scalv)
                    if pair >= 4:
                        p.wait("gpsimd", "sT", scv[pair - 4])
                    p.emit("gpsimd", lambda e, pair=pair, d=scstbf2[psl], s=scaler: e.dma_start(
                        out=d[:, :], in_=s[2 * pair:2 * pair + 2, :]), inc=csem, inc_by=16)
                    scbv[pair] = p.cnt[csem]
                    for ch in range(NCH):
                        dep = 4 * pair + ch
                        pS = ps[dep % 4]
                        pT = ps[4 + dep % 4]
                        p.wait("tensor", ["gc", "gc1", "gc2", "gc3"][psl], scbv[pair])
                        if pair == 0 and ch == 0:
                            p.wait("tensor", "mv", mv2)
                            p.wait_cur("tensor", "sG")
                        if dep >= 4:
                            p.wait("tensor", "sV", normv2[dep - 4])
                        p.emit("tensor", lambda e, pair=pair, ch=ch, pS=pS, s=scstbf2[psl]: e.matmul(
                            pS[:, :], ind2g_b[:, 128 * pair:128 * pair + 128],
                            s[:, 512 * ch:512 * ch + 512],
                            start=True, stop=True))
                        p.emit("tensor", lambda e, pair=pair, ch=ch, pT=pT, m2=minv2: e.matmul(
                            pT[:, :], gb2_b[:, 128 * pair:128 * pair + 128],
                            m2[:, 512 * ch:512 * ch + 512],
                            start=True, stop=True), inc="sT")
                        scv[pair] = p.cnt["sT"]
                        p.wait_cur("vector", "sT")
                        ip = ch % 2
                        p.drain("vector")
                        p.emit("vector", lambda e, pair=pair, ch=ch, d=ntmp[ip], pS=pS, qkvT=qkvT: e.tensor_tensor(
                            d[:, :], qkvT[:, 2048 * pair + 512 * ch:2048 * pair + 512 * ch + 512],
                            pS[:, :], ALU.mult))
                        p.drain("vector")
                        p.emit("vector", lambda e, pair=pair, ch=ch, s=ntmp[ip], pT=pT, qkvT=qkvT: e.tensor_tensor(
                            qkvT[:, 2048 * pair + 512 * ch:2048 * pair + 512 * ch + 512],
                            s[:, :], pT[:, :], ALU.subtract), inc="sV")
                        normv2[dep] = p.cnt["sV"]
                p.barrier(f"a2-{b}")
        with contextlib.ExitStack() as a3:
            sba3 = lambda n, s_, d: a3.enter_context(nc.sbuf_tensor(n, list(s_), d))
            rbuf = [sba3(f"a3_r{b}_{i}", (128, 512), F32) for i in range(2)]
            xbuf = [sba3(f"a3_x{b}_{i}", (128, 512), F32) for i in range(2)]
            argb = [sba3(f"a3_a{b}_{i}", (128, 512), F32) for i in range(2)]
            hbuf = [sba3(f"a3_h{b}_{i}", (128, 512), F32) for i in range(2)]
            gbuf = [sba3(f"a3_g{b}_{i}", (128, 512), BF16) for i in range(2)]
            sgb = [sba3(f"a3_sg{b}_{i}", (128, 512), F32) for i in range(2)]
            outb = [sba3(f"a3_o{b}_{i}", (128, 1024), F32) for i in range(2)]

            p.wait_cur("tensor", "sG")

            rxv = {}
            consv = {}
            outdma = {}

            def emit_rx_dma(pre):
                ptt, pec = pre // 2, pre % 2
                ppar = pre % 2
                prsem = "rcA" if ppar == 0 else "rcB"
                pxsem = "xA" if ppar == 0 else "xB"
                if pre == 0:
                    p.wait("sync", "sCC", 1)
                if pre >= 2:
                    p.wait("sync", "sV", consv[pre - 2])
                p.emit("sync", lambda e, ptt=ptt, pec=pec, d=rbuf[ppar]: e.dma_start(
                    out=d[:, :], in_=cc_out[128 * ptt:128 * ptt + 128, 512 * pec:512 * pec + 512]),
                    inc=prsem, inc_by=16)
                p.emit("sync", lambda e, ptt=ptt, pec=pec, d=xbuf[ppar], s=xbd[b]: e.dma_start(
                    out=d[:, :], in_=s[128 * ptt:128 * ptt + 128, 512 * pec:512 * pec + 512]),
                    inc=pxsem, inc_by=16)
                rxv[pre] = (p.cnt[prsem], p.cnt[pxsem])

            emit_rx_dma(0)
            emit_rx_dma(1)
            for it in range(STL * 2):
                tt, ec = it // 2, it % 2
                bpar = it % 2
                opar = tt % 2
                osem = "oA" if opar == 0 else "oB"
                rsem = "rcA" if bpar == 0 else "rcB"
                xsem = "xA" if bpar == 0 else "xB"

                pS = ps[it % 4]
                pG = ps[4 + it % 4]
                if it >= 4:
                    p.wait("tensor", "sV", consv[it - 4])
                for i in range(ET):
                    p.emit("tensor", lambda e, tt=tt, ec=ec, i=i, pS=pS, qkvT=qkvT, wosh=wosh: e.matmul(
                        pS[:, :], qkvT[:, 2048 * i + 128 * tt:2048 * i + 128 * tt + 128],
                        wosh[:, E * i + 512 * ec:E * i + 512 * ec + 512],
                        start=(i == 0), stop=(i == ET - 1)),
                        inc=("sT" if i == ET - 1 else None))
                for i in range(ET):
                    p.emit("tensor", lambda e, tt=tt, ec=ec, i=i, pG=pG, xnT=xnT, wwsh=wwsh: e.matmul(
                        pG[:, :], xnT[:, 2048 * i + 128 * tt:2048 * i + 128 * tt + 128],
                        wwsh[:, E * i + 512 * ec:E * i + 512 * ec + 512],
                        start=(i == 0), stop=(i == ET - 1)),
                        inc=("sT" if i == ET - 1 else None))
                gdone = p.cnt["sT"]

                p.wait("vector", "sT", gdone)
                p.wait("vector", rsem, rxv[it][0])
                p.drain("vector")
                p.emit("vector", lambda e, d=argb[bpar], pG=pG, r=rbuf[bpar]: e.tensor_tensor(
                    d[:, :], pG[:, :], r[:, :], ALU.add), inc="sV")
                p.wait_cur("scalar", "sV")
                p.emit("scalar", lambda e, d=hbuf[bpar], s=argb[bpar]: e.activation(
                    d[:, :], s[:, :], AF.Tanh))
                p.drain("scalar")
                p.emit("scalar", lambda e, d=gbuf[bpar], s=hbuf[bpar]: e.activation(
                    d[:, :], s[:, :], AF.Silu), inc="sS")
                p.wait_cur("vector", "sS")
                p.wait("vector", xsem, rxv[it][1])
                p.emit("vector", lambda e, d=sgb[bpar], pS=pS, g=gbuf[bpar]: e.tensor_tensor(
                    d[:, :], pS[:, :], g[:, :], ALU.mult))
                p.drain("vector")
                if ec == 0 and tt >= 2:
                    p.wait("vector", osem, outdma[tt - 2])
                p.emit("vector", lambda e, d=outb[opar], ec=ec, s=sgb[bpar], x=xbuf[bpar]: e.tensor_tensor(
                    d[:, 512 * ec:512 * ec + 512], s[:, :],
                    x[:, :], ALU.add), inc="sV")
                consv[it] = p.cnt["sV"]

                if ec == 1:
                    p.wait("sync", "sV", consv[it])
                    p.emit("sync", lambda e, tt=tt, d=outd[b], s=outb[opar]: e.dma_start(
                        out=d[128 * tt:128 * tt + 128, :], in_=s[:, :]),
                        inc=osem, inc_by=16)
                    outdma[tt] = p.cnt[osem]
                if it + 2 < STL * 2:
                    emit_rx_dma(it + 2)
            p.wait_cur("gpsimd", "oA")
            p.wait_cur("gpsimd", "oB")
            p.barrier(f"a3-{b}")
        w3.close()
        a12x.close()

    p.wait_cur("gpsimd", "oA")
    p.wait_cur("gpsimd", "oB")
    p.barrier("end")


def _finalize(nc, p):
    with nc.Block() as block:
        for eng in ENGINES:
            def runner(e, eng=eng):
                for op in p.ops[eng]:
                    op(e)
            getattr(block, eng)(runner)



def _host_inputs(core, full):
    x = full["L0_attn_x"]
    d = {}
    d["xb"] = np.ascontiguousarray(x[BPC * core:BPC * (core + 1)])
    d["xbT"] = np.ascontiguousarray(x[BPC * core:BPC * (core + 1)].transpose(0, 2, 1))
    d["xjT"] = np.ascontiguousarray(x[:, JPC * core:JPC * (core + 1), :].transpose(0, 2, 1))
    tr = {"wfT": "L0_ffn_wf", "wiT": "L0_ffn_wi", "wcT": "L0_ffn_wc",
          "ufT": "L0_ffn_uf", "uiT": "L0_ffn_ui", "ucT": "L0_ffn_uc",
          "uoT": "L0_ffn_uo", "woT": "L0_ffn_wo",
          "WqT": "L0_attn_Wq", "WkT": "L0_attn_Wk", "WoT": "L0_attn_Wo"}
    for k, v in tr.items():
        d[k] = np.ascontiguousarray(full[v].T)
    perm = np.arange(E).reshape(D, H).T.reshape(-1)
    d["WvTp"] = np.ascontiguousarray(full["L0_attn_Wv"].T[:, perm])
    def cols(v):
        return np.ascontiguousarray(np.asarray(v).reshape(ET, 128).T)

    d["bf"] = cols(full["L0_ffn_bf"])
    d["bi"] = cols(full["L0_ffn_bi"])
    d["bc"] = cols(full["L0_ffn_bc"])
    d["in_c"] = cols(full["L0_ffn_in_c"])
    d["gamma"] = cols(full["L0_attn_gamma"])
    d["beta"] = cols(full["L0_attn_beta"])
    d["bo"] = np.asarray(full["L0_ffn_bo"]).reshape(1, E)
    sidx = np.arange(128)[:, None]
    tidx = np.arange(128)[None, :]
    trimask = (sidx <= tidx).astype(np.float32)
    d["ident"] = np.eye(128, dtype=np.float32)
    am = np.zeros((128, 4, 512), np.float32)
    for j in range(4):
        am[:, j, 0:128 * j] = -100.0
        am[:, j, 128 * j:128 * j + 128] = -100.0 * (1.0 - trimask)
    d["addmask"] = am.reshape(128, 2048)
    ind2 = np.zeros((128, 2), np.float32)
    ind2[0:64, 0] = 1.0
    ind2[64:128, 1] = 1.0
    d["ind2"] = ind2
    d["ind2t"] = np.ascontiguousarray(ind2.T)
    d["ones16"] = np.ones((16, 1), np.float32)
    gam = np.asarray(full["L0_attn_gamma"], np.float32)
    bet = np.asarray(full["L0_attn_beta"], np.float32)
    d["gb2"] = np.ascontiguousarray(np.stack([gam, bet]))
    ind2g = np.zeros((2, E), np.float32)
    for pr in range(NP):
        for hh in range(2):
            seg = slice(128 * pr + 64 * hh, 128 * pr + 64 * hh + 64)
            ind2g[hh, seg] = gam[seg]
    d["ind2g"] = ind2g
    d["negrow"] = np.full((1, S), -1.0, np.float32)
    return {k: np.ascontiguousarray(v.astype(np.float32)) for k, v in d.items()}


def kernel(**inputs):
    nc = build_program("full")
    in_maps = [_host_inputs(c, inputs) for c in range(NC)]
    res = run_bass_kernel_spmd(nc, in_maps, core_ids=list(range(NC)))
    return np.concatenate([np.asarray(r["out"]) for r in res.results], axis=0)
